# version 01
# baseline (speedup 1.0000x reference)
import contextlib
import numpy as np
import concourse.bass as bass
import concourse.mybir as mybir
from concourse.bass_utils import run_bass_kernel_spmd

F32 = mybir.dt.float32
BF16 = mybir.dt.bfloat16
AF = mybir.ActivationFunctionType
ALU = mybir.AluOpType
AX = mybir.AxisListType.X

T = 2048
D = 1024
NTT = T // 128
NDT = D // 128
UP = 1344
UPP = 1408
E_MOE = 32
EPS = 1e-5
MAGIC = float(np.float32(3 << 22))
TWO_PI = float(2 * np.pi)

RIEMANN = (14.134725142, 21.022039639, 25.01085758, 30.424876126, 32.935061588,
           37.586178159, 40.918719012, 43.327073281, 48.005150881, 49.773832478,
           52.970321478, 56.446247697, 59.347044003, 60.831778525, 65.112544048,
           67.079810529, 69.546401711, 72.067157674, 75.704690699, 77.144840069,
           79.33737502, 82.910380854, 84.735492981, 87.425274613, 88.809111208,
           92.491899271, 94.651344041, 95.870634228, 98.831194218, 101.317851006,
           103.72553804, 105.446623052)

_UPTO = "all"


def _perm_qk():
    perm = np.zeros(1024, np.int64)
    for P in range(8):
        b = P * 128
        h0, h1 = 2 * P, 2 * P + 1
        perm[b +  0:b + 32] = h0 * 64 + np.arange(32) * 2
        perm[b + 32:b + 64] = h1 * 64 + np.arange(32) * 2
        perm[b + 64:b + 96] = h0 * 64 + np.arange(32) * 2 + 1
        perm[b + 96:b +128] = h1 * 64 + np.arange(32) * 2 + 1
    return perm


def _prep(inp):
    C = np.ascontiguousarray
    A = lambda k: np.asarray(inp[k], np.float32)
    o = {}
    pq = _perm_qk()
    wq = A("L0_attn_w_qkv")
    o["w_qkv"] = C(np.concatenate([wq[:, 0:1024][:, pq], wq[:, 1024:2048][:, pq],
                                   wq[:, 2048:3072]], axis=1))
    o["w_o"] = C(A("L0_attn_w_o"))
    o["log_scale"] = C(A("L0_attn_log_scale").reshape(16, 1))
    o["w1T"] = C(A("L0_ffn_W1").T)
    o["b1"] = C(A("L0_ffn_b1").reshape(32, 128).T)
    o["w2T"] = C(A("L0_ffn_W2").T)
    o["b2"] = C(A("L0_ffn_b2").reshape(8, 128).T)
    o["ln1_w"] = C(A("L1_attn_ln1_w").reshape(4, 8, 128).transpose(2, 0, 1))
    o["ln2_w"] = C(A("L1_attn_ln2_w").reshape(4, 8, 128).transpose(2, 0, 1))
    o["gn_w"] = C(A("L1_attn_gn_w"))
    o["post_w"] = C(A("L1_attn_post_w").reshape(1, 1024))
    o["conv_w"] = C(A("L1_attn_conv_w").reshape(4, 8, 128, 4).transpose(2, 0, 1, 3))
    o["conv_b"] = C(A("L1_attn_conv_b").reshape(4, 8, 128).transpose(2, 0, 1))
    for nm in ("wi", "wf", "wz", "wo"):
        o[nm + "T"] = C(A(f"L1_attn_{nm}").transpose(0, 1, 3, 2))
    o["rker"] = C(A("L1_attn_rker").reshape(4, 4, 256, 1024))
    cb = A("L1_attn_cell_b").reshape(4, 4, 4, 2, 128)
    o["cell_b"] = C(cb.transpose(4, 0, 2, 1, 3).reshape(128, 4, 32))
    fu = A("L1_attn_ff_up")
    fup = np.zeros((4, 1024, 2 * UPP), np.float32)
    fup[:, :, :UP] = fu[:, :, :UP]
    fup[:, :, UPP:UPP + UP] = fu[:, :, UP:]
    o["ff_up"] = C(fup)
    fd = A("L1_attn_ff_down")
    fdp = np.zeros((4, UPP, 1024), np.float32)
    fdp[:, :UP, :] = fd
    o["ff_down"] = C(fdp)
    o["router_wT"] = C(A("L1_ffn_router_w").T)
    o["router_b"] = C(A("L1_ffn_router_bias").reshape(32, 1))
    o["gate_wT"] = C(A("L1_ffn_gate_w").transpose(0, 2, 1))
    o["up_wT"] = C(A("L1_ffn_up_w").transpose(0, 2, 1))
    o["down_wT"] = C(A("L1_ffn_down_w").transpose(0, 2, 1))
    o["sgate_wT"] = C(A("L1_ffn_sgate_w").T)
    o["sup_wT"] = C(A("L1_ffn_sup_w").T)
    o["sdown_wT"] = C(A("L1_ffn_sdown_w").T)
    g = np.asarray(RIEMANN[:32], np.float32)
    invf = (1.0 / (g / g[0])).astype(np.float32)
    o["c_invf"] = C((np.tile(invf, 4) / np.float32(2 * np.pi)).reshape(128, 1))
    return o



def build_program(upto="all"):
    st = contextlib.ExitStack()
    nc = bass.Bass("TRN2", target_bir_lowering=False,
                   detect_race_conditions=False)

    din = lambda name, shape: nc.dram_tensor(name, shape, F32, kind="ExternalInput")

    x_hbm = din("x", [T, D])
    w_qkv_h = din("w_qkv", [D, 3 * D])
    w_o_h = din("w_o", [D, D])
    log_scale_h = din("log_scale", [16, 1])
    w1T_h = din("w1T", [D, 4096]); b1_h = din("b1", [128, 32])
    w2T_h = din("w2T", [4096, D]); b2_h = din("b2", [128, 8])
    ln1_wh = din("ln1_w", [128, 4, 8]); ln2_wh = din("ln2_w", [128, 4, 8])
    gn_wh = din("gn_w", [4, D]); post_wh = din("post_w", [1, D])
    conv_wh = din("conv_w", [128, 4, 8, 4]); conv_bh = din("conv_b", [128, 4, 8])
    wT_h = {nm: din(nm + "T", [4, 4, 256, 256]) for nm in ("wi", "wf", "wz", "wo")}
    rker_h = din("rker", [4, 4, 256, 1024])
    cell_bh = din("cell_b", [128, 4, 32])
    ff_up_h = din("ff_up", [4, D, 2 * UPP]); ff_down_h = din("ff_down", [4, UPP, D])
    router_wT_h = din("router_wT", [D, 32]); router_b_h = din("router_b", [32, 1])
    gate_wT_h = din("gate_wT", [E_MOE, D, 128]); up_wT_h = din("up_wT", [E_MOE, D, 128])
    down_wT_h = din("down_wT", [E_MOE, 128, D])
    sgate_wT_h = din("sgate_wT", [D, 512]); sup_wT_h = din("sup_wT", [D, 512])
    sdown_wT_h = din("sdown_wT", [512, D])
    c_invf_h = din("c_invf", [128, 1])

    out_hbm = nc.dram_tensor("out", [T, D], F32, kind="ExternalOutput")
    wq1_hbm = nc.dram_tensor("wq1i", [D, 4096], BF16, kind="Internal")
    wq2_hbm = nc.dram_tensor("wq2i", [4096, D], BF16, kind="Internal")
    xresv_hbm = nc.dram_tensor("xresvi", [128, NTT * D], F32, kind="Internal")
    ffu_hbm = nc.dram_tensor("ffui", [D, 2 * UPP], BF16, kind="Internal")
    ffd_hbm = nc.dram_tensor("ffdi", [UPP, D], BF16, kind="Internal")

    sbt = lambda n, s, d: st.enter_context(nc.sbuf_tensor(n, s, d))
    psbt = lambda n, s: st.enter_context(nc.psum_tensor(n, s, F32))

    x_res = sbt("x_res", [128, NTT, D], F32)
    xnT = sbt("xnT", [128, NDT, T], BF16)
    tokA = sbt("tokA", [128, 16384], BF16)
    arena = sbt("arena", [128, 16384], BF16)
    fscA = sbt("fscA", [128, 2048], F32)
    fscB = sbt("fscB", [128, 2048], F32)
    smalls = sbt("smalls", [128, 128], F32)
    consts = sbt("consts", [128, 64], F32)
    cbf = sbt("cbf", [128, 520], BF16)
    masks = sbt("masks", [128, 4, 512], BF16)
    wsm = sbt("wsm", [128, 2, 512], BF16)
    hbuf = sbt("hbuf", [128, 8], BF16)
    sst = sbt("sst", [128, 16], F32)
    sscr = sbt("sscr", [128, 64], F32)
    qx1 = sbt("qx1", [64, T], BF16); qx2 = sbt("qx2", [64, T], BF16)
    kx1 = sbt("kx1", [64, T], BF16); kx2 = sbt("kx2", [64, T], BF16)
    ytok = tokA.rearrange("p (a d) -> p a d", a=NTT)
    junkf = arena.bitcast(F32)[0:64, 5120:7168]
    attnT = tokA.rearrange("p (a d) -> p a d", a=NDT)
    junk = arena[:, 15360:16384]

    psA = psbt("psA", [128, 1024])
    psZ = psbt("psZ", [128, 512])
    psC = psbt("psC", [128, 512])
    psB = psbt("psB", [128, 2048])

    class Sem:
        def __init__(self, name):
            self.h = st.enter_context(nc.semaphore(name))
            self.n = 0
    sL, sT, sM, sV, sS, sG, sO = (Sem(x) for x in
                                  ("ld", "tr", "mm", "vv", "ss", "gg", "sto"))
    sMs, sVs, sSs = (Sem(x) for x in ("mms", "vvs", "sss"))
    BAR = nc.all_engine_barrier
    ts = bass.ts

    scan_regs_holder = []

    @contextlib.contextmanager
    def ForiScan(start, end, engines):
        if not scan_regs_holder:
            scan_regs_holder.append(nc.alloc_registers("scan_i", engines=engines))
        regs = scan_regs_holder[0]
        loop_id = nc.next_id()
        loop_start = f"scan_fori_{loop_id}_loop"
        loop_end = f"scan_fori_{loop_id}_end"
        nc.regs_mov(regs, start)
        nc.br(loop_start, engines=engines)
        box = []
        with nc.body(loop_start, valid_engines=engines):
            i = nc.snap(regs, min_val=start, max_val=end - 1)
            box.append(i)
            yield i
            nc.regs_alu(regs, regs, 1, op=ALU.add)
            nc.br_lt(regs, end, on_true=loop_start, on_false=loop_end,
                     engines=engines)
        nc.switch_bb(loop_end)
        for h in box[0].val.handles:
            nc.free_register(h)

    ENG3 = [mybir.EngineType.PE, mybir.EngineType.Activation, mybir.EngineType.DVE]
    slp = nc.alloc_registers("slp_i", engines=ENG3)
    r_vT = nc.alloc_registers("r_vT", engines=[mybir.EngineType.PE])
    r_vS = nc.alloc_registers("r_vS", engines=[mybir.EngineType.Activation])
    r_vV = nc.alloc_registers("r_vV", engines=[mybir.EngineType.DVE])
    r_mV = nc.alloc_registers("r_mV", engines=[mybir.EngineType.DVE])
    r_sV = nc.alloc_registers("r_sV", engines=[mybir.EngineType.DVE])
    r_iw = nc.alloc_registers("r_iw", engines=[mybir.EngineType.DVE])
    r_ih = nc.alloc_registers("r_ih", engines=[mybir.EngineType.DVE])
    iv_w = bass.RuntimeValue(r_iw, min_val=0, max_val=255)
    iv_h = bass.RuntimeValue(r_ih, min_val=0, max_val=255)

    def free_tmps(id0, id1):
        for k in range(id0, id1 + 1):
            for eng, enm in (("DVE", mybir.EngineType.DVE),
                             ("PE", mybir.EngineType.PE),
                             ("Activation", mybir.EngineType.Activation)):
                try:
                    nc.free_register(bass.RegisterHandle(name=f"{eng}_tmp_{k}",
                                                         engine=enm))
                except Exception:
                    pass

    with nc.Block() as blk:
        @blk.gpsimd
        def _(g):
            g.dma_start(out=x_res[:],
                        in_=x_hbm.ap().rearrange("(a p) d -> p a d", p=128)
                        ).then_inc(sL.h, 16)
            g.dma_start(out=consts[:, 1:2], in_=c_invf_h.ap()).then_inc(sL.h, 16)
            g.memset(consts[:, 0:1], 1.0)
            g.memset(consts[:, 3:4], 1e-5)
            g.memset(consts[0:1, 8:9], 1.0)
            g.memset(cbf[:, 0:1], 1.0)
            g.memset(cbf[0:1, 1:65], 1.0)
            g.memset(cbf[0:1, 192:320] if False else cbf[0:1, 1:129], 1.0)
            g.memset(cbf[:, 260:388], 1.0)
            g.memset(cbf[:, 392:520], 0.0)
            for j in range(4):
                inst = g.iota(fscA[:, j * 512:(j + 1) * 512], [[1, 512]],
                              base=-128 * j, channel_multiplier=-1,
                              allow_small_or_imprecise_dtypes=True)
                inst.then_inc(sG.h, 1)
            inst = g.iota(fscB[:, 0:128], [[1, 128]], base=0, channel_multiplier=-1,
                          allow_small_or_imprecise_dtypes=True)
            inst.then_inc(sG.h, 1)
        @blk.vector
        def _(v):
            v.wait_ge(sL.h, 32)
            v.wait_ge(sG.h, 5)
            for j in range(4):
                v.tensor_scalar(masks[:, j, :], fscA[:, j * 512:(j + 1) * 512],
                                -0.5, None, op0=ALU.is_ge)
            v.tensor_scalar(fscB[:, 128:256], fscB[:, 0:128], -0.5, None,
                            op0=ALU.is_ge)
            v.tensor_scalar(fscB[:, 0:128], fscB[:, 0:128], 0.5, None,
                            op0=ALU.is_le)
            v.tensor_mul(cbf[:, 130:258], fscB[:, 128:256], fscB[:, 0:128])
    sL.n = 32; sG.n = 5
    BAR()
    ones_col_bf = cbf[:, 0:1]
    ones_row64 = cbf[0:1, 1:65]
    ones_row128 = cbf[0:1, 1:129]
    ident128 = cbf[:, 130:258]
    allones = cbf[:, 260:388]
    zeros128 = cbf[:, 392:520]

    def emit_ln(src3, n_tiles=NTT, apply_mode="copy", dst3=None):
        musum = smalls[:, 0:n_tiles]
        s2c = smalls[:, 16:16 + n_tiles]
        muc = smalls[:, 32:32 + n_tiles]
        varc = smalls[:, 48:48 + n_tiles]
        rstdc = smalls[:, 64:64 + n_tiles]
        lvc = smalls[:, 80:80 + n_tiles]
        bV, bS = sV.n, sS.n
        with nc.Block() as blk:
            @blk.vector
            def _(v):
                for tt in range(n_tiles):
                    inst = v.reduce_sum(musum[:, tt:tt + 1], src3[:, tt, 0:D], axis=AX)
                inst.then_inc(sV.h, 1)
            @blk.scalar
            def _(s):
                for tt in range(n_tiles):
                    inst = s.activation(junk[:, 0:D], src3[:, tt, 0:D], AF.Square,
                                        accum_out=s2c[:, tt:tt + 1])
                inst.then_inc(sS.h, 1)
        with nc.Block() as blk:
            @blk.vector
            def _(v):
                v.wait_ge(sV.h, bV + 1)
                v.wait_ge(sS.h, bS + 1)
                v.tensor_scalar_mul(muc, musum, 1.0 / D).then_inc(sV.h, 1)
                v.wait_ge(sV.h, bV + 2)
                v.tensor_mul(varc, muc, muc).then_inc(sV.h, 1)
                v.wait_ge(sV.h, bV + 3)
                v.scalar_tensor_tensor(varc, s2c, 1.0 / D, varc,
                                       op0=ALU.mult, op1=ALU.subtract).then_inc(sV.h, 1)
            @blk.scalar
            def _(s):
                s.wait_ge(sV.h, bV + 4)
                s.activation(lvc, varc, AF.Ln, bias=consts[:, 3:4]).then_inc(sS.h, 1)
                s.wait_ge(sS.h, bS + 2)
                s.activation(rstdc, lvc, AF.Exp, scale=-0.5).then_inc(sS.h, 1)
        bV2, bS2 = bV + 4, bS + 3
        with nc.Block() as blk:
            @blk.vector
            def _(v):
                v.wait_ge(sS.h, bS2)
                for tt in range(n_tiles):
                    if apply_mode == "copy":
                        inst = v.tensor_scalar(dst3[:, tt, 0:D], src3[:, tt, 0:D],
                                               muc[:, tt:tt + 1], rstdc[:, tt:tt + 1],
                                               op0=ALU.subtract, op1=ALU.mult)
                    elif apply_mode == "inplace":
                        inst = v.tensor_scalar(src3[:, tt, 0:D], src3[:, tt, 0:D],
                                               muc[:, tt:tt + 1], rstdc[:, tt:tt + 1],
                                               op0=ALU.subtract, op1=ALU.mult)
                    else:
                        v.tensor_scalar(fscA[:, 0:D], src3[:, tt, 0:D],
                                        muc[:, tt:tt + 1], rstdc[:, tt:tt + 1],
                                        op0=ALU.subtract, op1=ALU.mult)
                        inst = v.tensor_add(x_res[:, tt, :], x_res[:, tt, :],
                                            fscA[:, 0:D])
                inst.then_inc(sV.h, 1)
        sV.n = bV2 + 1
        sS.n = bS2
        BAR()

    def emit_t2d(src3, dstT, n_tt=NTT, n_dt=NDT, dst_col_off=0):
        bT = sT.n
        with nc.Block() as blk:
            @blk.sync
            def _(sy):
                for tt in range(n_tt):
                    sy.dma_start_transpose(
                        dstT[:, :, dst_col_off + tt * 128:dst_col_off + (tt + 1) * 128],
                        src3[:, tt, 0:n_dt * 128]).then_inc(sT.h, 16)
            @blk.vector
            def _(v):
                v.wait_ge(sT.h, bT + 16 * n_tt)
        sT.n = bT + 16 * n_tt
        BAR()

    def emit_d2t(srcT, dst3, n_dt=NDT, n_tt=NTT):
        bT = sT.n
        with nc.Block() as blk:
            @blk.sync
            def _(sy):
                for dt in range(n_dt):
                    sy.dma_start_transpose(
                        dst3[:, :, dt * 128:(dt + 1) * 128],
                        srcT[:, dt, 0:n_tt * 128]).then_inc(sT.h, 16)
            @blk.vector
            def _(v):
                v.wait_ge(sT.h, bT + 16 * n_dt)
        sT.n = bT + 16 * n_dt
        BAR()

    def emit_l0_attn():
        emit_ln(x_res, apply_mode="copy", dst3=ytok)
        emit_t2d(ytok, xnT)

        bL, bS_, bV_ = sL.n, sS.n, sV.n
        with nc.Block() as blk:
            @blk.gpsimd
            def _(g):
                for Pp in range(8):
                    g.dma_start(out=smalls[0:32, 96 + Pp:97 + Pp],
                                in_=log_scale_h.ap()[2 * Pp:2 * Pp + 1, :]
                                .to_broadcast([32, 1])).then_inc(sL.h, 16)
                    g.dma_start(out=smalls[32:64, 96 + Pp:97 + Pp],
                                in_=log_scale_h.ap()[2 * Pp + 1:2 * Pp + 2, :]
                                .to_broadcast([32, 1])).then_inc(sL.h, 16)
            @blk.scalar
            def _(s):
                s.wait_ge(sL.h, bL + 256)
                inst = s.activation(smalls[0:64, 96:104], smalls[0:64, 96:104], AF.Exp)
                inst.then_inc(sS.h, 1)
            @blk.vector
            def _(v):
                v.wait_ge(sS.h, bS_ + 1)
                for Pp in range(8):
                    inst = v.tensor_mul(smalls[0:64, 112 + Pp:113 + Pp],
                                        smalls[0:64, 96 + Pp:97 + Pp],
                                        consts[0:64, 1:2])
                inst.then_inc(sV.h, 1)
        sL.n = bL + 256; sS.n = bS_ + 1; sV.n = bV_ + 1
        BAR()

        wqkv_sb = arena[:, 0:6144].rearrange("p (b s k c) -> p b s k c", b=2, s=3, k=8)
        ropeP = arena[0:64, 6144:10240].rearrange("p (a t) -> p a t", a=2)
        vtokbuf = arena[:, 10240:12288].rearrange("p (k c) -> p k c", k=16)
        vstage = arena[:, 12288:14336]
        uevac = arena[0:64, 12288:14336]
        expS = arena[:, 14336:14848]
        rzrow = arena[0:1, 14848:15360]
        attnT_e = attnT
        attnT_o_full = attnT
        cs_t = ropeP[:, 0, :]; sn_t = ropeP[:, 1, :]

        for Pp in range(8):
            wq_v = wqkv_sb[:, Pp % 2, :, :, :]
            bL = sL.n
            with nc.Block() as blk:
                @blk.gpsimd
                def _(g):
                    if Pp >= 2:
                        g.wait_ge(sM.h, sM.n)
                    for si, base in enumerate((Pp * 128, 1024 + Pp * 128,
                                               2048 + Pp * 128)):
                        g.dma_start(out=wq_v[:, si, :, :],
                                    in_=w_qkv_h.ap()[:, base:base + 128]
                                    .rearrange("(k p) c -> p k c", p=128)
                                    ).then_inc(sL.h, 16)
            sL.n = bL + 48

            coef = smalls[0:64, 112 + Pp:113 + Pp]
            bV_, bS_, bG_ = sV.n, sS.n, sG.n
            with nc.Block() as blk:
                @blk.gpsimd
                def _(g):
                    if Pp:
                        g.wait_ge(sV.h, bV_)
                    g.iota(fscA[0:64, 0:T], [[1, T]], base=0, channel_multiplier=0,
                           allow_small_or_imprecise_dtypes=True).then_inc(sG.h, 1)
                @blk.vector
                def _(v):
                    if Pp:
                        v.wait_ge(sS.h, bS_)
                    v.wait_ge(sG.h, bG_ + 1)
                    fr = fscA[0:64, 0:T]
                    fs = fscB[0:64, 0:T]
                    v.tensor_scalar_mul(fr, fr, coef)
                    v.tensor_scalar(fs, fr, MAGIC, -MAGIC, op0=ALU.add, op1=ALU.add)
                    v.tensor_sub(fs, fr, fs).then_inc(sV.h, 1)
                    v.tensor_scalar_add(fr, fr, 0.25)
                    v.wait_ge(sS.h, bS_ + 1)
                    v.tensor_scalar(fs, fr, MAGIC, -MAGIC, op0=ALU.add, op1=ALU.add)
                    v.tensor_sub(fs, fr, fs).then_inc(sV.h, 1)
                @blk.scalar
                def _(s):
                    s.wait_ge(sV.h, bV_ + 1)
                    s.activation(sn_t, fscB[0:64, 0:T], AF.Sin, scale=TWO_PI
                                 ).then_inc(sS.h, 1)
                    s.wait_ge(sV.h, bV_ + 2)
                    s.activation(cs_t, fscB[0:64, 0:T], AF.Sin, scale=TWO_PI
                                 ).then_inc(sS.h, 1)
            sV.n = bV_ + 2; sS.n = bS_ + 2; sG.n = bG_ + 1
            BAR()

            for si in range(3):
                bM, bV_, bS_ = sM.n, sV.n, sS.n
                with nc.Block() as blk:
                    if si < 2:
                        d1, d2 = (qx1, qx2) if si == 0 else (kx1, kx2)
                        @blk.tensor
                        def _(t):
                            t.wait_ge(sL.h, bL + 48)
                            for half in range(2):
                                if half == 1:
                                    t.wait_ge(sV.h, bV_ + 1)
                                for tq in range(4):
                                    for kt in range(8):
                                        inst = t.matmul(
                                            psB[0:64, tq * 512:(tq + 1) * 512],
                                            wq_v[:, si, kt, half * 64:half * 64 + 64],
                                            xnT[:, kt, tq * 512:(tq + 1) * 512],
                                            start=(kt == 0), stop=(kt == 7))
                                inst.then_inc(sM.h, 1)
                        @blk.vector
                        def _(v):
                            mcs = fscA[0:64, 0:T]
                            msn = fscB[0:64, 0:T]
                            v.wait_ge(sM.h, bM + 1)
                            v.tensor_mul(mcs, psB[0:64, 0:T], cs_t)
                            v.tensor_mul(msn, psB[0:64, 0:T], sn_t).then_inc(sV.h, 1)
                            v.wait_ge(sM.h, bM + 2)
                            v.tensor_mul(junkf[0:64, 0:T], psB[0:64, 0:T], sn_t)
                            v.tensor_sub(d1[:, :], mcs, junkf[0:64, 0:T])
                            v.tensor_mul(junkf[0:64, 0:T], psB[0:64, 0:T], cs_t)
                            v.tensor_add(d2[:, :], msn, junkf[0:64, 0:T]).then_inc(sV.h, 1)
                        sM.n = bM + 2
                        sV.n = bV_ + 2
                    else:
                        bT = sT.n
                        @blk.tensor
                        def _(t):
                            t.wait_ge(sL.h, bL + 48)
                            for tq in range(4):
                                for kt in range(8):
                                    inst = t.matmul(
                                        psB[:, tq * 512:(tq + 1) * 512],
                                        wq_v[:, 2, kt, :],
                                        xnT[:, kt, tq * 512:(tq + 1) * 512],
                                        start=(kt == 0), stop=(kt == 7))
                            inst.then_inc(sM.h, 1)
                        @blk.vector
                        def _(v):
                            v.wait_ge(sM.h, bM + 1)
                            v.tensor_copy(vstage[:, :], psB[:, 0:T]).then_inc(sV.h, 1)
                        @blk.sync
                        def _(sy):
                            sy.wait_ge(sV.h, bV_ + 1)
                            for hh in range(2):
                                sy.dma_start_transpose(
                                    vtokbuf[:, :, hh * 64:hh * 64 + 64],
                                    vstage[hh * 64:hh * 64 + 64, :]).then_inc(sT.h, 16)
                        sM.n = bM + 1
                        sV.n = bV_ + 1
                        sT.n = bT + 32
                BAR()

            for hh in range(2):
                rowlo, rowhi = (0, 64) if hh == 0 else (64, 128)
                r0 = hh * 32
                bM, bS_, bV_ = sM.n, sS.n, sV.n
                m_smm = {}; m_av = {}; m_bcast = {}
                s_exp = {}; s_rz = {}; s_uev = {}
                v_mask = {}; v_fin = {}
                mc = sc = vc = 0
                J = 0
                for qc in range(4):
                    for kt in range(4 * (qc + 1)):
                        mc += 1; m_smm[J] = mc
                        mc += 1; m_av[J] = mc
                        sc += 1; s_exp[J] = sc
                        if kt >= 4 * qc:
                            vc += 1; v_mask[J] = vc
                        J += 1
                    sc += 1; s_rz[qc] = sc
                    sc += 1; s_uev[qc] = sc
                    mc += 1; m_bcast[qc] = mc
                    vc += 1; v_fin[qc] = vc
                with nc.Block() as blk:
                    @blk.tensor
                    def _(t):
                        t.wait_ge(sT.h, sT.n)
                        J = 0
                        for qc in range(4):
                            ktmax = 4 * (qc + 1)
                            for kt in range(ktmax):
                                pss = psA[:, (J % 2) * 512:(J % 2) * 512 + 512]
                                if J >= 2:
                                    t.wait_ge(sS.h, bS_ + s_exp[J - 2])
                                t.matmul(pss,
                                         kx1[r0:r0 + 32, kt * 128:(kt + 1) * 128],
                                         qx1[r0:r0 + 32, qc * 512:(qc + 1) * 512],
                                         start=True, stop=False)
                                t.matmul(pss,
                                         kx2[r0:r0 + 32, kt * 128:(kt + 1) * 128],
                                         qx2[r0:r0 + 32, qc * 512:(qc + 1) * 512],
                                         start=False, stop=True).then_inc(sM.h, 1)
                                if J in v_mask:
                                    t.wait_ge(sV.h, bV_ + v_mask[J])
                                else:
                                    t.wait_ge(sS.h, bS_ + s_exp[J])
                                t.matmul(psB[:, qc * 512:(qc + 1) * 512],
                                         vtokbuf[:, kt, :],
                                         expS[:, :],
                                         start=(kt == 0), stop=(kt == ktmax - 1))
                                t.matmul(psZ[0:1, 0:512], ones_col_bf, expS[:, :],
                                         start=(kt == 0), stop=(kt == ktmax - 1)
                                         ).then_inc(sM.h, 1)
                                J += 1
                            t.wait_ge(sS.h, bS_ + s_rz[qc])
                            if qc:
                                t.wait_ge(sV.h, bV_ + v_fin[qc - 1])
                            t.matmul(psC[:, 0:512], ones_row128, rzrow[:, :],
                                     start=True, stop=True).then_inc(sM.h, 1)
                    @blk.scalar
                    def _(s):
                        J = 0
                        for qc in range(4):
                            ktmax = 4 * (qc + 1)
                            for kt in range(ktmax):
                                w = m_smm[J] if J == 0 else max(m_smm[J], m_av[J - 1])
                                s.wait_ge(sM.h, bM + w)
                                s.activation(expS[:, :],
                                             psA[:, (J % 2) * 512:(J % 2) * 512 + 512],
                                             AF.Exp, scale=0.125).then_inc(sS.h, 1)
                                J += 1
                            s.wait_ge(sM.h, bM + m_av[J - 1])
                            s.activation(fscA[0:1, 0:512], psZ[0:1, 0:512], AF.Ln)
                            if qc:
                                s.wait_ge(sM.h, bM + m_bcast[qc - 1])
                            s.activation(rzrow, fscA[0:1, 0:512], AF.Exp, scale=-1.0
                                         ).then_inc(sS.h, 1)
                            s.activation(
                                arena[rowlo:rowhi, 12288 + qc * 512:12288 + (qc + 1) * 512],
                                psB[rowlo:rowhi, qc * 512:(qc + 1) * 512],
                                AF.Identity).then_inc(sS.h, 1)
                    @blk.vector
                    def _(v):
                        J = 0
                        for qc in range(4):
                            ktmax = 4 * (qc + 1)
                            for kt in range(ktmax):
                                if J in v_mask:
                                    v.wait_ge(sS.h, bS_ + s_exp[J])
                                    v.tensor_mul(expS[:, :], expS[:, :],
                                                 masks[:, kt - 4 * qc, :]
                                                 ).then_inc(sV.h, 1)
                                J += 1
                            v.wait_ge(sM.h, bM + m_bcast[qc])
                            v.wait_ge(sS.h, bS_ + s_uev[qc])
                            v.tensor_mul(attnT[rowlo:rowhi, Pp, qc * 512:(qc + 1) * 512],
                                         arena[rowlo:rowhi,
                                               12288 + qc * 512:12288 + (qc + 1) * 512],
                                         psC[rowlo:rowhi, 0:512]).then_inc(sV.h, 1)
                sM.n = bM + mc
                sS.n = bS_ + sc
                sV.n = bV_ + vc
                BAR()
        BAR()

        wo_e = arena[0:64, 0:8192].rearrange("p (s c) -> p s c", s=8)
        wo_o = arena[64:128, 0:8192].rearrange("p (s c) -> p s c", s=8)
        wstage = arena[:, 8192:8704]
        bL = sL.n
        with nc.Block() as blk:
            @blk.gpsimd
            def _(g):
                for Pp2 in range(8):
                    g.dma_start(out=wo_e[:, Pp2, :],
                                in_=w_o_h.ap()[(2 * Pp2) * 64:(2 * Pp2 + 1) * 64, :]
                                ).then_inc(sL.h, 16)
                    g.dma_start(out=wo_o[:, Pp2, :],
                                in_=w_o_h.ap()[(2 * Pp2 + 1) * 64:(2 * Pp2 + 2) * 64, :]
                                ).then_inc(sL.h, 16)
        sL.n = bL + 256
        bM, bS_, bV_ = sM.n, sS.n, sV.n
        with nc.Block() as blk:
            @blk.tensor
            def _(t):
                t.wait_ge(sL.h, bL + 256)
                cnt = 0
                for tq in range(4):
                    for dmt in range(8):
                        pe = psA[:, (cnt % 2) * 512:(cnt % 2) * 512 + 512]
                        po = psZ[:, 0:512] if cnt % 2 == 0 else psC[:, 0:512]
                        if cnt >= 2:
                            t.wait_ge(sS.h, bS_ + cnt - 1)
                            t.wait_ge(sV.h, bV_ + cnt - 1)
                        for Pp2 in range(8):
                            inst = t.matmul(pe, wo_e[:, Pp2, dmt * 128:(dmt + 1) * 128],
                                            attnT[0:64, Pp2, tq * 512:(tq + 1) * 512],
                                            start=(Pp2 == 0), stop=(Pp2 == 7))
                        inst.then_inc(sM.h, 1)
                        for Pp2 in range(8):
                            inst = t.matmul(po, wo_o[:, Pp2, dmt * 128:(dmt + 1) * 128],
                                            attnT[64:128, Pp2, tq * 512:(tq + 1) * 512],
                                            start=(Pp2 == 0), stop=(Pp2 == 7))
                        inst.then_inc(sM.h, 1)
                        cnt += 1
            @blk.scalar
            def _(s):
                cnt = 0
                for tq in range(4):
                    for dmt in range(8):
                        pe = psA[:, (cnt % 2) * 512:(cnt % 2) * 512 + 512]
                        s.wait_ge(sM.h, bM + 2 * cnt + 1)
                        if cnt >= 1:
                            s.wait_ge(sV.h, bV_ + cnt)
                        s.activation(wstage, pe, AF.Identity).then_inc(sS.h, 1)
                        cnt += 1
            @blk.vector
            def _(v):
                cnt = 0
                for tq in range(4):
                    for dmt in range(8):
                        po = psZ[:, 0:512] if cnt % 2 == 0 else psC[:, 0:512]
                        v.wait_ge(sM.h, bM + 2 * cnt + 2)
                        v.wait_ge(sS.h, bS_ + cnt + 1)
                        v.tensor_add(xnT[:, dmt, tq * 512:(tq + 1) * 512],
                                     wstage, po).then_inc(sV.h, 1)
                        cnt += 1
        sM.n = bM + 64
        sS.n = bS_ + 32
        sV.n = bV_ + 32
        BAR()
        emit_d2t(xnT, ytok)
        emit_ln(ytok, apply_mode="resid")

    def emit_l0_ffn():
        emit_ln(x_res, apply_mode="copy", dst3=ytok)
        emit_t2d(ytok, xnT)
        arF = arena.bitcast(F32)
        w1v = lambda kt: w1T_h.ap()[(kt // 2) * 128:(kt // 2 + 1) * 128,
                                    (kt % 2) * 2048:(kt % 2 + 1) * 2048]
        wq1v = lambda kt: wq1_hbm.ap()[(kt // 2) * 128:(kt // 2 + 1) * 128,
                                       (kt % 2) * 2048:(kt % 2 + 1) * 2048]
        w2v = lambda kt: w2T_h.ap()[kt * 128:(kt + 1) * 128, :]
        wq2v = lambda kt: wq2_hbm.ap()[kt * 128:(kt + 1) * 128, :]
        abs1 = fscB[:, 0:16]
        abs2 = fscB[:, 16:48]

        def gamma_pass(wv, n_t, cols, absc):
            bL, bS_ = sL.n, sS.n
            with nc.Block() as blk:
                @blk.gpsimd
                def _(g):
                    for kt in range(n_t):
                        if kt >= 2:
                            g.wait_ge(sS.h, bS_ + kt - 1)
                        g.dma_start(out=arF[:, (kt % 2) * cols:(kt % 2) * cols + cols],
                                    in_=wv(kt)).then_inc(sL.h, 16)
                @blk.scalar
                def _(s):
                    for kt in range(n_t):
                        s.wait_ge(sL.h, bL + 16 * (kt + 1))
                        s.activation(fscA[:, 0:cols],
                                     arF[:, (kt % 2) * cols:(kt % 2) * cols + cols],
                                     AF.Abs, accum_out=absc[:, kt:kt + 1]
                                     ).then_inc(sS.h, 1)
            sL.n = bL + 16 * n_t; sS.n = bS_ + n_t
            BAR()
        gamma_pass(w1v, 16, 2048, abs1)
        gamma_pass(w2v, 32, 1024, abs2)

        srep1 = smalls[:, 126:127]
        srep2 = smalls[:, 127:128]
        bV, bS_, bM = sV.n, sS.n, sM.n
        with nc.Block() as blk:
            @blk.vector
            def _(v):
                v.reduce_sum(fscB[:, 60:61], abs1, axis=AX)
                v.reduce_sum(fscB[:, 62:63], abs2, axis=AX).then_inc(sV.h, 1)
        sV.n = bV + 1
        BAR()
        bV, bS_, bM = sV.n, sS.n, sM.n
        with nc.Block() as blk:
            @blk.tensor
            def _(t):
                t.matmul(psZ[0:1, 0:1], consts[:, 0:1], fscB[:, 60:61],
                         start=True, stop=True)
                t.matmul(psZ[0:1, 1:2], consts[:, 0:1], fscB[:, 62:63],
                         start=True, stop=True).then_inc(sM.h, 1)
            @blk.scalar
            def _(s):
                s.wait_ge(sM.h, bM + 1)
                s.activation(fscB[0:1, 65:67], psZ[0:1, 0:2], AF.Identity,
                             scale=1.0 / (1024.0 * 4096.0), bias=consts[0:1, 3:4]
                             ).then_inc(sS.h, 1)
            @blk.vector
            def _(v):
                v.wait_ge(sS.h, bS_ + 1)
                v.reciprocal(fscB[0:1, 67:69], fscB[0:1, 65:67]).then_inc(sV.h, 1)
        sM.n = bM + 1; sS.n = bS_ + 1; sV.n = bV + 1
        BAR()
        bV, bM = sV.n, sM.n
        with nc.Block() as blk:
            @blk.tensor
            def _(t):
                t.matmul(psZ[:, 4:6], consts[0:1, 8:9].to_broadcast([1, 128]),
                         fscB[0:1, 67:69], start=True, stop=True).then_inc(sM.h, 1)
            @blk.vector
            def _(v):
                v.wait_ge(sM.h, bM + 1)
                v.tensor_copy(srep1, psZ[:, 4:5])
                v.tensor_copy(srep2, psZ[:, 5:6]).then_inc(sV.h, 1)
        sM.n = bM + 1; sV.n = bV + 1
        BAR()

        def quant_pass(wv, n_t, cols, wqv, srep):
            bL, bV = sL.n, sV.n
            with nc.Block() as blk:
                @blk.gpsimd
                def _(g):
                    for kt in range(n_t):
                        if kt >= 2:
                            g.wait_ge(sV.h, bV + kt - 1)
                        g.dma_start(out=arF[:, (kt % 2) * cols:(kt % 2) * cols + cols],
                                    in_=wv(kt)).then_inc(sL.h, 16)
                        g.wait_ge(sV.h, bV + kt + 1)
                        g.dma_start(out=wqv(kt),
                                    in_=arena[:, 8192 + (kt % 2) * cols:
                                              8192 + (kt % 2) * cols + cols]
                                    ).then_inc(sL.h, 16)
                @blk.vector
                def _(v):
                    for kt in range(n_t):
                        v.wait_ge(sL.h, bL + 32 * kt + 16)
                        src_ = arF[:, (kt % 2) * cols:(kt % 2) * cols + cols]
                        v.tensor_scalar(fscA[:, 0:cols], src_, srep, 0.5,
                                        op0=ALU.mult, op1=ALU.is_ge)
                        v.tensor_scalar(fscB[:, 0:cols], src_, srep, -0.5,
                                        op0=ALU.mult, op1=ALU.is_le)
                        v.tensor_sub(arena[:, 8192 + (kt % 2) * cols:
                                           8192 + (kt % 2) * cols + cols],
                                     fscA[:, 0:cols], fscB[:, 0:cols]
                                     ).then_inc(sV.h, 1)
            with nc.Block() as blk2:
                @blk2.gpsimd
                def _(g):
                    g.wait_ge(sL.h, bL + 32 * n_t)
            sL.n = bL + 32 * n_t; sV.n = bV + n_t
            BAR()
        quant_pass(w1v, 16, 2048, wq1v, srep1)
        quant_pass(w2v, 32, 1024, wq2v, srep2)

        b1_sb = fscB[:, 0:32]
        b2_sb = fscB[:, 32:40]
        bL = sL.n
        with nc.Block() as blk:
            @blk.gpsimd
            def _(g):
                g.dma_start(out=b1_sb, in_=b1_h.ap()).then_inc(sL.h, 16)
                g.dma_start(out=b2_sb, in_=b2_h.ap()).then_inc(sL.h, 16)
            @blk.vector
            def _(v):
                v.wait_ge(sL.h, bL + 32)
        sL.n = bL + 32
        BAR()

        h_sb = arena[:, 0:8192].rearrange("p (a c) -> p a c", a=32)
        wq1s = arena[:, 8192:16384].rearrange("p (b k c) -> p b k c", b=2, k=8)
        tc = 256
        for tqi in range(8):
            t0 = tqi * tc
            bL, bM, bS_ = sL.n, sM.n, sS.n
            with nc.Block() as blk:
                @blk.gpsimd
                def _(g):
                    for fc in range(8):
                        if fc >= 1:
                            g.wait_ge(sM.h, bM + 4 * fc)
                        g.dma_start(out=wq1s[:, fc % 2, :, :],
                                    in_=wq1_hbm.ap()[:, fc * 512:(fc + 1) * 512]
                                    .rearrange("(k p) c -> p k c", p=128)
                                    ).then_inc(sL.h, 16)
                @blk.tensor
                def _(t):
                    for fc in range(8):
                        t.wait_ge(sL.h, bL + 16 * (fc + 1))
                        if fc >= 1:
                            t.wait_ge(sS.h, bS_ + 4 * fc)
                        for mt in range(4):
                            for kt in range(8):
                                inst = t.matmul(
                                    psB[:, mt * 512:mt * 512 + tc],
                                    wq1s[:, fc % 2, kt, mt * 128:(mt + 1) * 128],
                                    xnT[:, kt, t0:t0 + tc],
                                    start=(kt == 0), stop=(kt == 7))
                            inst.then_inc(sM.h, 1)
                @blk.scalar
                def _(s):
                    for fc in range(8):
                        for mt in range(4):
                            s.wait_ge(sM.h, bM + 4 * fc + mt + 1)
                            s.activation(h_sb[:, fc * 4 + mt, 0:tc],
                                         psB[:, mt * 512:mt * 512 + tc],
                                         AF.Relu,
                                         bias=b1_sb[:, fc * 4 + mt:fc * 4 + mt + 1]
                                         ).then_inc(sS.h, 1)
            sL.n = bL + 128; sM.n = bM + 32; sS.n = bS_ + 32
            BAR()
            bL, bM, bS_ = sL.n, sM.n, sS.n
            with nc.Block() as blk:
                @blk.gpsimd
                def _(g):
                    for dh in range(2):
                        for kt in range(32):
                            j = dh * 32 + kt
                            if j >= 1:
                                g.wait_ge(sM.h, bM + j)
                            g.dma_start(out=wsm[:, j % 2, 0:512],
                                        in_=wq2_hbm.ap()[kt * 128:(kt + 1) * 128,
                                                         dh * 512:(dh + 1) * 512]
                                        ).then_inc(sL.h, 16)
                @blk.tensor
                def _(t):
                    for dh in range(2):
                        if dh == 1:
                            t.wait_ge(sS.h, bS_ + 4)
                        for kt in range(32):
                            j = dh * 32 + kt
                            t.wait_ge(sL.h, bL + 16 * (j + 1))
                            for mt in range(4):
                                inst = t.matmul(
                                    psB[:, mt * 512:mt * 512 + tc],
                                    wsm[:, j % 2, mt * 128:(mt + 1) * 128],
                                    h_sb[:, kt, 0:tc],
                                    start=(kt == 0), stop=(kt == 31))
                            inst.then_inc(sM.h, 1)
                @blk.scalar
                def _(s):
                    for dh in range(2):
                        s.wait_ge(sM.h, bM + 32 * (dh + 1))
                        for mt in range(4):
                            dmt = dh * 4 + mt
                            s.activation(xnT[:, dmt, t0:t0 + tc],
                                         psB[:, mt * 512:mt * 512 + tc],
                                         AF.Identity, bias=b2_sb[:, dmt:dmt + 1]
                                         ).then_inc(sS.h, 1)
            sL.n = bL + 16 * 64; sM.n = bM + 64; sS.n = bS_ + 8
            BAR()
        emit_d2t(xnT, ytok)
        emit_ln(ytok, apply_mode="resid")

    def emit_qdbg():
        bL, bV = sL.n, sV.n
        with nc.Block() as blk:
            @blk.vector
            def _(v):
                for a in range(16):
                    v.tensor_copy(x_res[:, a, :],
                                  xnT[:, a // 2, (a % 2) * 1024:(a % 2 + 1) * 1024]
                                  ).then_inc(sV.h, 1)
        sV.n = bV + 16
        BAR()

    def emit_l1_ffn():
        emit_ln(x_res, apply_mode="copy", dst3=ytok)
        emit_t2d(ytok, xnT)
        rwT = arena[:, 0:256].rearrange("p (k c) -> p k c", k=8)
        wnTc = arena[0:64, 256:384]
        wG = arena[:, 768:1792].rearrange("p (k c) -> p k c", k=8)
        wU = arena[:, 1792:2816].rearrange("p (k c) -> p k c", k=8)
        wD = arena[:, 2816:3840]
        sgT = arena[:, 3840:7936].rearrange("p (k c) -> p k c", k=8)
        suT = arena[:, 7936:12032].rearrange("p (k c) -> p k c", k=8)
        sdT = arena[:, 12032:16128].rearrange("p (k c) -> p k c", k=4)
        rlog = qx1[0:32, :]
        wn_tok = tokA[:, 0:512].rearrange("p (a c) -> p a c", a=16)

        bL, bM, bS_ = sL.n, sM.n, sS.n
        with nc.Block() as blk:
            @blk.gpsimd
            def _(g):
                g.dma_start(out=rwT[:, :, :],
                            in_=router_wT_h.ap().rearrange("(k p) c -> p k c", p=128)
                            ).then_inc(sL.h, 16)
                g.dma_start(out=smalls[0:32, 8:9], in_=router_b_h.ap()).then_inc(sL.h, 16)
            @blk.tensor
            def _(t):
                t.wait_ge(sL.h, bL + 32)
                for tq in range(4):
                    inst = t.matmul(psB[0:32, tq * 512:(tq + 1) * 512], rwT[:, 0, :]
                                    if False else rwT[:, 0, :],
                                    xnT[:, 0, tq * 512:(tq + 1) * 512],
                                    start=True, stop=False)
                    for kt in range(1, 8):
                        inst = t.matmul(psB[0:32, tq * 512:(tq + 1) * 512],
                                        rwT[:, kt, :],
                                        xnT[:, kt, tq * 512:(tq + 1) * 512],
                                        start=False, stop=(kt == 7))
                inst.then_inc(sM.h, 1)
            @blk.scalar
            def _(s):
                s.wait_ge(sM.h, bM + 1)
                s.activation(rlog[:, :], psB[0:32, 0:T], AF.Identity,
                             bias=smalls[0:32, 8:9]).then_inc(sS.h, 1)
        sL.n = bL + 32; sM.n = bM + 1; sS.n = bS_ + 1
        BAR()
        bT = sT.n
        with nc.Block() as blk:
            @blk.sync
            def _(sy):
                sy.dma_start_transpose(wn_tok[:, :, 0:32], rlog[:, :]).then_inc(sT.h, 16)
            @blk.vector
            def _(v):
                v.wait_ge(sT.h, bT + 16)
        sT.n = bT + 16
        BAR()
        mxc = smalls[:, 0:1]; nmx = smalls[:, 1:2]; sc_ = smalls[:, 2:3]
        rs_ = smalls[:, 3:4]; v4 = smalls[:, 4:5]; ws_ = smalls[:, 5:6]
        rws = smalls[:, 6:7]
        pfull = fscA[:, 0:32]; m8 = fscA[:, 32:40]; wfull = fscA[:, 40:72]
        bV, bS_ = sV.n, sS.n
        with nc.Block() as blk:
            @blk.vector
            def _(v):
                for tt in range(NTT):
                    vb = sV.n + 10 * tt
                for tt in range(NTT):
                    base = bV + 7 * tt
                    v.reduce_max(mxc, wn_tok[:, tt, 0:32], axis=AX)
                    v.wait_ge(sV.h, 0)
                    v.tensor_scalar_mul(nmx, mxc, -1.0).then_inc(sV.h, 1)
                    v.wait_ge(sS.h, bS_ + 2 * tt + 1)
                    v.reciprocal(rs_, sc_).then_inc(sV.h, 1)
                    v.wait_ge(sV.h, base + 2)
                    v.tensor_scalar_mul(pfull, fscB[:, 0:32], rs_
                                        ).then_inc(sV.h, 1)
                    v.wait_ge(sV.h, base + 3)
                    v.max(out=m8, in_=pfull).then_inc(sV.h, 1)
                    v.wait_ge(sV.h, base + 4)
                    v.tensor_scalar(wfull, pfull, m8[:, 3:4], None,
                                    op0=ALU.is_ge).then_inc(sV.h, 1)
                    v.wait_ge(sV.h, base + 5)
                    v.tensor_mul(wfull, wfull, pfull)
                    v.reduce_sum(ws_, wfull, axis=AX).then_inc(sV.h, 1)
                    v.wait_ge(sV.h, base + 6)
                    v.tensor_scalar_max(ws_, ws_, 1e-12)
                    v.reciprocal(rws, ws_).then_inc(sV.h, 1)
                    v.wait_ge(sV.h, base + 7)
                    v.tensor_scalar_mul(wn_tok[:, tt, 0:32], wfull, rws)
            @blk.scalar
            def _(s):
                for tt in range(NTT):
                    base = bV + 7 * tt
                    s.wait_ge(sV.h, base + 1)
                    s.activation(fscB[:, 0:32], wn_tok[:, tt, 0:32], AF.Exp,
                                 bias=nmx, accum_out=sc_).then_inc(sS.h, 1)
                    s.wait_ge(sV.h, base + 3)
                    s.activation(junk[:, 0:1], sc_, AF.Identity).then_inc(sS.h, 1)
        sV.n = bV + 7 * NTT
        sS.n = bS_ + 2 * NTT
        BAR()

        bL = sL.n
        with nc.Block() as blk:
            @blk.gpsimd
            def _(g):
                g.dma_start(out=sgT[:, :, :],
                            in_=sgate_wT_h.ap().rearrange("(k p) c -> p k c", p=128)
                            ).then_inc(sL.h, 16)
                g.dma_start(out=suT[:, :, :],
                            in_=sup_wT_h.ap().rearrange("(k p) c -> p k c", p=128)
                            ).then_inc(sL.h, 16)
                g.dma_start(out=sdT[:, :, :],
                            in_=sdown_wT_h.ap().rearrange("(k p) c -> p k c", p=128)
                            ).then_inc(sL.h, 16)
        sL.n = bL + 48
        BAR()

        tc = 256
        sg_sb = tokA[:, 512:768]
        hu_sb = fscB[:, 0:256]
        hp_sb = tokA[:, 768:1024]
        for ci in range(8):
            t0 = ci * tc
            ttA, ttB = 2 * ci, 2 * ci + 1
            bM, bV = sM.n, sV.n
            with nc.Block() as blk:
                @blk.tensor
                def _(t):
                    psAb = psA.bitcast(BF16)
                    t.transpose(psAb[0:64, 0:128],
                                wn_tok[:, ttA, 0:32].rearrange("p c -> p c")
                                if False else
                                tokA[:, ttA * 32:ttA * 32 + 64],
                                ident128).then_inc(sM.h, 1)
                @blk.vector
                def _(v):
                    v.wait_ge(sM.h, bM + 1)
                    v.tensor_copy(wnTc, psA.bitcast(BF16)[0:64, 0:128]
                                  ).then_inc(sV.h, 1)
            sM.n = bM + 1; sV.n = bV + 1
            BAR()
            bL, bM, bS_, bV = sL.n, sM.n, sS.n, sV.n
            with nc.Block() as blk:
                @blk.gpsimd
                def _(g):
                    for e in range(E_MOE):
                        if e >= 1:
                            g.wait_ge(sM.h, bM + 3 * e)
                        g.dma_start(out=wG[:, :, :],
                                    in_=gate_wT_h.ap()[e].rearrange(
                                        "(k p) c -> p k c", p=128)).then_inc(sL.h, 16)
                        g.dma_start(out=wU[:, :, :],
                                    in_=up_wT_h.ap()[e].rearrange(
                                        "(k p) c -> p k c", p=128)).then_inc(sL.h, 16)
                        g.dma_start(out=wD[:, :],
                                    in_=down_wT_h.ap()[e]).then_inc(sL.h, 16)
                @blk.tensor
                def _(t):
                    for bank in range(4):
                        t.matmul(psB[:, bank * 512:(bank + 1) * 512], zeros128,
                                 xnT[:, 0, 0:512], start=True, stop=False,
                                 skip_group_check=True)
                    for e in range(E_MOE):
                        t.wait_ge(sL.h, bL + 48 * (e + 1))
                        if e >= 1:
                            t.wait_ge(sS.h, bS_ + e)
                            t.wait_ge(sV.h, bV + 3 * e)
                        for kt in range(8):
                            t.matmul(psA[:, 0:tc], wG[:, kt, :],
                                     xnT[:, kt, t0:t0 + tc],
                                     start=(kt == 0), stop=(kt == 7))
                        for kt in range(8):
                            inst = t.matmul(psA[:, 512:512 + tc], wU[:, kt, :],
                                            xnT[:, kt, t0:t0 + tc],
                                            start=(kt == 0), stop=(kt == 7))
                        inst.then_inc(sM.h, 1)
                        t.matmul(psC[:, 0:128],
                                 ident128[0:64, e:e + 1].to_broadcast([64, 128]),
                                 wnTc, start=True, stop=True)
                        inst = t.matmul(psC[:, 128:256],
                                        ident128[0:64, 32 + e:33 + e
                                                 ].to_broadcast([64, 128]),
                                        wnTc, start=True, stop=True)
                        inst.then_inc(sM.h, 1)
                        t.wait_ge(sV.h, bV + 3 * e + 3)
                        for mt in range(8):
                            inst = t.matmul(psB[:, mt * tc:(mt + 1) * tc],
                                            wD[:, mt * 128:(mt + 1) * 128],
                                            hp_sb[:, 0:tc],
                                            start=False, stop=False,
                                            skip_group_check=True)
                        inst.then_inc(sM.h, 1)
                    t.wait_ge(sS.h, bS_ + 32 + 0)
                    t.wait_ge(sV.h, bV + 96)
                    for st4 in range(4):
                        for kt in range(8):
                            inst = t.matmul(psA[:, st4 * tc:(st4 + 1) * tc]
                                            if st4 < 4 else None,
                                            sgT[:, kt, st4 * 128:(st4 + 1) * 128],
                                            xnT[:, kt, t0:t0 + tc],
                                            start=(kt == 0), stop=(kt == 7))
                    inst.then_inc(sM.h, 1)
                    for st4 in range(4):
                        ps_u = (psZ if st4 < 2 else psC)
                        for kt in range(8):
                            inst = t.matmul(ps_u[:, (st4 % 2) * tc:(st4 % 2 + 1) * tc],
                                            suT[:, kt, st4 * 128:(st4 + 1) * 128],
                                            xnT[:, kt, t0:t0 + tc],
                                            start=(kt == 0), stop=(kt == 7))
                    inst.then_inc(sM.h, 1)
                    for st4 in range(4):
                        t.wait_ge(sV.h, bV + 96 + 2 * (st4 + 1))
                        for mt in range(8):
                            inst = t.matmul(psB[:, mt * tc:(mt + 1) * tc],
                                            sdT[:, st4, mt * 128:(mt + 1) * 128],
                                            hp_sb[:, 0:tc],
                                            start=False, stop=(st4 == 3),
                                            skip_group_check=True)
                        inst.then_inc(sM.h, 1)
                @blk.scalar
                def _(s):
                    for e in range(E_MOE):
                        s.wait_ge(sM.h, bM + 3 * e + 1)
                        s.activation(sg_sb[:, 0:tc], psA[:, 0:tc], AF.Silu
                                     ).then_inc(sS.h, 1)
                    s.wait_ge(sM.h, bM + 96 + 1)
                    for st4 in range(4):
                        s.activation(sg_sb[:, 0:tc], psA[:, st4 * tc:(st4 + 1) * tc],
                                     AF.Silu).then_inc(sS.h, 1)
                        s.wait_ge(sV.h, bV + 96 + 2 * (st4 + 1))
                    s.wait_ge(sM.h, bM + 96 + 6)
                    for mt in range(8):
                        s.activation(xnT[:, mt, t0:t0 + tc],
                                     psB[:, mt * tc:(mt + 1) * tc], AF.Identity
                                     ).then_inc(sS.h, 1)
                @blk.vector
                def _(v):
                    for e in range(E_MOE):
                        vb = bV + 3 * e
                        v.wait_ge(sS.h, bS_ + e + 1)
                        v.tensor_mul(hu_sb, sg_sb[:, 0:tc], psA[:, 512:512 + tc]
                                     ).then_inc(sV.h, 1)
                        v.wait_ge(sV.h, vb + 1)
                        v.wait_ge(sM.h, bM + 3 * e + 2)
                        v.tensor_mul(fscA[:, 0:tc], hu_sb,
                                     psC[:, 0:tc]).then_inc(sV.h, 1)
                        v.wait_ge(sV.h, vb + 2)
                        v.tensor_copy(hp_sb[:, 0:tc], fscA[:, 0:tc]).then_inc(sV.h, 1)
                    v.wait_ge(sM.h, bM + 96 + 2)
                    for st4 in range(4):
                        vb = bV + 96 + 2 * st4
                        v.wait_ge(sS.h, bS_ + 32 + st4 + 1)
                        ps_u = (psZ if st4 < 2 else psC)
                        v.tensor_mul(fscA[:, 0:tc], sg_sb[:, 0:tc],
                                     ps_u[:, (st4 % 2) * tc:(st4 % 2 + 1) * tc]
                                     ).then_inc(sV.h, 1)
                        v.wait_ge(sV.h, vb + 1)
                        v.tensor_copy(hp_sb[:, 0:tc], fscA[:, 0:tc]).then_inc(sV.h, 1)
            sL.n = bL + 48 * 32
            sM.n = bM + 96 + 6
            sS.n = bS_ + 32 + 4 + 8
            sV.n = bV + 96 + 8
            BAR()
        emit_d2t(xnT, ytok)
        emit_ln(ytok, apply_mode="resid")

    def emit_l1_attn():
        fscBb = fscB.bitcast(BF16)
        hns = fscBb[:, 2048:4096].rearrange("p (a t) -> p a t", a=8)
        hv = hns.rearrange("p a t -> p t a")
        gnwr = masks[:, 0:2, :].rearrange("p a c -> p (a c)")
        mf = masks.bitcast(F32).rearrange("p a c -> p (a c)")
        cvw = mf[:, 512:640].rearrange("p (a b k) -> p a b k", a=4, b=8)
        cvb = mf[:, 640:672].rearrange("p (a b) -> p a b", a=4)
        lnw_sb = mf[:, 672:736].rearrange("p (a b c) -> p a b c", a=2, b=4)
        cbl = mf[:, 736:864].rearrange("p (a b) -> p a b", a=4)
        wsf = wsm.rearrange("p a c -> p (a c)")
        rkT = arena[:, 0:8192].rearrange("p (h k m c) -> p h k m c",
                                         h=4, k=2, m=8)
        wxb = arena[:, 8192:16384].rearrange("p (t c) -> p t c", t=256)
        hxT = tokA.rearrange("p (a t) -> p a t", a=NDT)
        EN = mybir.EngineType

        bL = sL.n
        with nc.Block() as blk:
            @blk.gpsimd
            def _(g):
                g.dma_start(out=xresv_hbm.ap(),
                            in_=x_res[:].rearrange("p a d -> p (a d)")
                            ).then_inc(sL.h, 16)
                g.dma_start(out=cvw[:], in_=conv_wh.ap()).then_inc(sL.h, 16)
                g.dma_start(out=cvb[:], in_=conv_bh.ap()).then_inc(sL.h, 16)
                g.dma_start(out=lnw_sb[:, 0, :, :], in_=ln1_wh.ap()).then_inc(sL.h, 16)
                g.dma_start(out=lnw_sb[:, 1, :, :], in_=ln2_wh.ap()).then_inc(sL.h, 16)
                g.dma_start(out=cbl[:], in_=cell_bh.ap()).then_inc(sL.h, 16)
                g.wait_ge(sL.h, bL + 96)
        sL.n = bL + 96
        BAR()
        emit_ln(x_res, apply_mode="inplace")

        for l in range(4):
            bL = sL.n
            with nc.Block() as blk:
                @blk.gpsimd
                def _(g):
                    g.dma_start(out=gnwr[:, :],
                                in_=gn_wh.ap()[l:l + 1, :].to_broadcast([128, D])
                                ).then_inc(sL.h, 16)
                    for h in range(4):
                        g.dma_start(out=rkT[:, h, :, :, :],
                                    in_=rker_h.ap()[l, h].rearrange(
                                        "(k p) (m c) -> p k m c", p=128, c=128)
                                    ).then_inc(sL.h, 16)
                    g.wait_ge(sL.h, bL + 80)
            sL.n = bL + 80
            BAR()

            emit_ln(x_res, apply_mode="copy", dst3=ytok)
            emit_t2d(ytok, xnT)
            bS_ = sS.n
            with nc.Block() as blk:
                @blk.scalar
                def _(s):
                    for dt in range(NDT):
                        inst = s.activation(xnT[:, dt, :], xnT[:, dt, :],
                                            AF.Identity,
                                            scale=lnw_sb[:, 0, l, dt:dt + 1])
                    inst.then_inc(sS.h, 1)
            sS.n = bS_ + 1
            BAR()

            bV, bS_, bG = sV.n, sS.n, sG.n
            with nc.Block() as blk:
                @blk.gpsimd
                def _(g):
                    g.memset(fscBb[:, 0:4], 0.0)
                    g.tensor_copy(consts[:, 0:1], consts[:, 0:1]).then_inc(sG.h, 1)
                @blk.vector
                def _(v):
                    v.wait_ge(sG.h, bG + 1)
                    for dt in range(NDT):
                        if dt:
                            v.wait_ge(sS.h, bS_ + dt)
                        v.tensor_copy(fscBb[:, 4:4 + T], xnT[:, dt, :])
                        v.tensor_scalar_mul(fscA[:, 0:T], fscBb[:, 1:1 + T],
                                            cvw[:, l, dt, 0:1])
                        for k in range(1, 4):
                            inst = v.scalar_tensor_tensor(
                                fscA[:, 0:T], fscBb[:, 1 + k:1 + k + T],
                                cvw[:, l, dt, k:k + 1], fscA[:, 0:T],
                                op0=ALU.mult, op1=ALU.add)
                        inst.then_inc(sV.h, 1)
                @blk.scalar
                def _(s):
                    for dt in range(NDT):
                        s.wait_ge(sV.h, bV + dt + 1)
                        s.activation(hxT[:, dt, :], fscA[:, 0:T], AF.Silu,
                                     bias=cvb[:, l, dt:dt + 1]).then_inc(sS.h, 1)
            sV.n = bV + NDT; sG.n = bG + 1; sS.n = bS_ + NDT
            BAR()

            bG = sG.n
            with nc.Block() as blk:
                @blk.gpsimd
                def _(g):
                    g.memset(sst[:], 0.0)
                    g.memset(hbuf[:], 0.0)
                    g.tensor_copy(consts[:, 0:1], consts[:, 0:1]).then_inc(sG.h, 1)
                @blk.vector
                def _(v):
                    v.wait_ge(sG.h, bG + 1)
            sG.n = bG + 1
            BAR()

            for ci in range(8):
                t0 = ci * 256
                bL, bM, bV = sL.n, sM.n, sV.n
                with nc.Block() as blk:
                    @blk.gpsimd
                    def _(g):
                        for h in range(4):
                            for gi_, wname in enumerate(("wi", "wf", "wz", "wo")):
                                idx = h * 4 + gi_
                                if idx >= 2:
                                    g.wait_ge(sM.h, bM + 2 * (idx - 1))
                                g.dma_start(
                                    out=wsf[:, (idx % 2) * 512:(idx % 2) * 512 + 512]
                                    .rearrange("p (k c) -> p k c", k=2),
                                    in_=wT_h[wname].ap()[l, h].rearrange(
                                        "(k p) c -> p k c", p=128)
                                    ).then_inc(sL.h, 16)
                    @blk.tensor
                    def _(t):
                        for h in range(4):
                            for gi_ in range(4):
                                idx = h * 4 + gi_
                                srcT = hxT if gi_ < 2 else xnT
                                t.wait_ge(sL.h, bL + 16 * (idx + 1))
                                if idx >= 2:
                                    t.wait_ge(sV.h, bV + 2 * (idx - 1))
                                pso = psA if idx % 2 == 0 else psB
                                wslot = (idx % 2) * 512
                                for ot in range(2):
                                    for kt in range(2):
                                        inst = t.matmul(
                                            pso[:, ot * 512:ot * 512 + 256],
                                            wsf[:, wslot + kt * 256 + ot * 128:
                                                wslot + kt * 256 + ot * 128 + 128],
                                            srcT[:, 2 * h + kt, t0:t0 + 256],
                                            start=(kt == 0), stop=(kt == 1))
                                    inst.then_inc(sM.h, 1)
                    @blk.vector
                    def _(v):
                        for h in range(4):
                            for gi_ in range(4):
                                idx = h * 4 + gi_
                                pso = psA if idx % 2 == 0 else psB
                                for ot in range(2):
                                    v.wait_ge(sM.h, bM + 2 * idx + ot + 1)
                                    col = gi_ * 8 + h * 2 + ot
                                    v.tensor_scalar_add(
                                        wxb[:, :, col],
                                        pso[:, ot * 512:ot * 512 + 256],
                                        cbl[:, l, col:col + 1]).then_inc(sV.h, 1)
                sL.n = bL + 256; sM.n = bM + 32; sV.n = bV + 32
                BAR()

                bM, bV, bS_ = sMs.n, sVs.n, sSs.n
                id0 = nc.next_id()
                ls, le = f"scan{id0}l", f"scan{id0}e"
                nc.regs_mov(slp, 0)
                nc.regs_mov(r_vT, bV); nc.regs_mov(r_vS, bV + 1)
                nc.regs_mov(r_vV, bV + 2)
                nc.regs_mov(r_mV, bM + 1); nc.regs_mov(r_sV, bS_ + 1)
                nc.regs_mov(r_iw, 0); nc.regs_mov(r_ih, 0)
                RVx = bass.RuntimeValue
                nc.br(ls, engines=ENG3)
                with nc.body(ls, valid_engines=ENG3):
                    nc.tensor.wait_ge(sVs.h, RVx(r_vT, min_val=bV,
                                                 max_val=bV + 2550))
                    for h in range(4):
                        for mt in range(8):
                            col = (mt // 2) * 8 + h * 2 + (mt % 2)
                            for kt in range(2):
                                inst = nc.tensor.matmul(
                                    psA[:, col:col + 1],
                                    rkT[:, h, kt, mt, :],
                                    hbuf[:, h * 2 + kt:h * 2 + kt + 1],
                                    start=(kt == 0), stop=(kt == 1))
                    inst.then_inc(sMs.h, 1)
                    nc.regs_alu(r_vT, r_vT, 10, op=ALU.add)
                    nc.vector.wait_ge(sMs.h, RVx(r_mV, min_val=bM + 1,
                                                 max_val=bM + 256))
                    nc.regs_alu(r_mV, r_mV, 1, op=ALU.add)
                    g_ = sscr[:, 0:32]
                    nc.vector.tensor_add(g_, psA[:, 0:32], wxb[:, iv_w, :]
                                         ).then_inc(sVs.h, 1)
                    nc.regs_alu(r_iw, r_iw, 1, op=ALU.add)
                    tF = sscr[:, 32:40]; eI = sscr[:, 40:48]
                    tz = sscr[:, 48:56]; tO = sscr[:, 56:64]
                    nc.scalar.wait_ge(sVs.h, RVx(r_vS, min_val=bV + 1,
                                                 max_val=bV + 2551))
                    nc.scalar.activation(tF, g_[:, 8:16], AF.Tanh, scale=0.5)
                    nc.scalar.activation(eI, g_[:, 0:8], AF.Exp)
                    nc.scalar.activation(tz, g_[:, 16:24], AF.Tanh)
                    inst = nc.scalar.activation(tO, g_[:, 24:32], AF.Tanh,
                                                scale=0.5)
                    inst.then_inc(sSs.h, 1)
                    nc.regs_alu(r_vS, r_vS, 10, op=ALU.add)
                    a_ = sst[:, 0:8]; E_ = sst[:, 8:16]
                    F_ = sscr[:, 0:8]; sg_ = sscr[:, 8:16]
                    Em = sscr[:, 16:24]; p_ = sscr[:, 24:32]
                    t2_ = sscr[:, 32:40]
                    nc.vector.wait_ge(sSs.h, RVx(r_sV, min_val=bS_ + 1,
                                                 max_val=bS_ + 256))
                    nc.regs_alu(r_sV, r_sV, 1, op=ALU.add)
                    nc.vector.tensor_scalar(F_, tF, 0.5, 0.5,
                                            op0=ALU.mult, op1=ALU.add)
                    nc.vector.tensor_scalar(sg_, tO, 0.5, 0.5,
                                            op0=ALU.mult, op1=ALU.add
                                            ).then_inc(sVs.h, 1)
                    vfence = RVx(r_vV, min_val=bV + 2, max_val=bV + 2559)
                    nc.vector.wait_ge(sVs.h, vfence)
                    nc.regs_alu(r_vV, r_vV, 1, op=ALU.add)
                    nc.vector.tensor_mul(Em, F_, E_).then_inc(sVs.h, 1)
                    nc.vector.wait_ge(sVs.h, vfence)
                    nc.regs_alu(r_vV, r_vV, 1, op=ALU.add)
                    nc.vector.tensor_add(E_, Em, eI).then_inc(sVs.h, 1)
                    nc.vector.wait_ge(sVs.h, vfence)
                    nc.regs_alu(r_vV, r_vV, 1, op=ALU.add)
                    nc.vector.tensor_sub(t2_, tz, a_)
                    nc.vector.reciprocal(Em, E_).then_inc(sVs.h, 1)
                    nc.vector.wait_ge(sVs.h, vfence)
                    nc.regs_alu(r_vV, r_vV, 1, op=ALU.add)
                    nc.vector.tensor_mul(p_, eI, Em).then_inc(sVs.h, 1)
                    nc.vector.wait_ge(sVs.h, vfence)
                    nc.regs_alu(r_vV, r_vV, 1, op=ALU.add)
                    nc.vector.tensor_mul(p_, p_, t2_).then_inc(sVs.h, 1)
                    nc.vector.wait_ge(sVs.h, vfence)
                    nc.regs_alu(r_vV, r_vV, 1, op=ALU.add)
                    nc.vector.tensor_add(a_, a_, p_).then_inc(sVs.h, 1)
                    nc.vector.wait_ge(sVs.h, vfence)
                    nc.regs_alu(r_vV, r_vV, 4, op=ALU.add)
                    nc.vector.tensor_mul(hbuf[:, :], sg_, a_).then_inc(sVs.h, 1)
                    inst = nc.vector.tensor_mul(hv[:, iv_h, :], sg_, a_)
                    inst.then_inc(sVs.h, 1)
                    nc.regs_alu(r_ih, r_ih, 1, op=ALU.add)
                    nc.regs_alu(slp, slp, 1, op=ALU.add)
                    nc.br_lt(slp, 256, on_true=ls, on_false=le, engines=ENG3)
                nc.switch_bb(le)
                free_tmps(id0, nc.next_id())
                sMs.n = bM + 256; sVs.n = bV + 2560; sSs.n = bS_ + 256
                with nc.Block() as blk:
                    @blk.vector
                    def _(v):
                        v.wait_ge(sVs.h, sVs.n)
                BAR()

                gnb = fscBb[:, 0:2048].rearrange("p (a c) -> p a c", a=2)
                bT = sT.n
                with nc.Block() as blk:
                    @blk.sync
                    def _(sy):
                        for h2o in range(8):
                            sy.dma_start_transpose(
                                gnb[:, :, h2o * 128:h2o * 128 + 128],
                                hns[:, h2o, :]).then_inc(sT.h, 16)
                    @blk.vector
                    def _(v):
                        v.wait_ge(sT.h, bT + 128)
                    @blk.scalar
                    def _(s):
                        s.wait_ge(sT.h, bT + 128)
                sT.n = bT + 128
                musum = smalls[:, 0:8]; s2c = smalls[:, 16:24]
                muc = smalls[:, 32:40]; varc = smalls[:, 48:56]
                rstdc = smalls[:, 64:72]; lvc = smalls[:, 80:88]
                bV, bS_ = sV.n, sS.n
                with nc.Block() as blk:
                    @blk.vector
                    def _(v):
                        for u in range(8):
                            tt2, h = u // 4, u % 4
                            inst = v.reduce_sum(musum[:, u:u + 1],
                                                gnb[:, tt2, h * 256:(h + 1) * 256],
                                                axis=AX)
                        inst.then_inc(sV.h, 1)
                    @blk.scalar
                    def _(s):
                        for u in range(8):
                            tt2, h = u // 4, u % 4
                            inst = s.activation(junk[:, 0:256],
                                                gnb[:, tt2, h * 256:(h + 1) * 256],
                                                AF.Square, accum_out=s2c[:, u:u + 1])
                        inst.then_inc(sS.h, 1)
                with nc.Block() as blk:
                    @blk.vector
                    def _(v):
                        v.wait_ge(sV.h, bV + 1)
                        v.wait_ge(sS.h, bS_ + 1)
                        v.tensor_scalar_mul(muc, musum, 1.0 / 256).then_inc(sV.h, 1)
                        v.wait_ge(sV.h, bV + 2)
                        v.tensor_mul(varc, muc, muc).then_inc(sV.h, 1)
                        v.wait_ge(sV.h, bV + 3)
                        v.scalar_tensor_tensor(varc, s2c, 1.0 / 256, varc,
                                               op0=ALU.mult, op1=ALU.subtract
                                               ).then_inc(sV.h, 1)
                    @blk.scalar
                    def _(s):
                        s.wait_ge(sV.h, bV + 4)
                        s.activation(lvc, varc, AF.Ln, bias=consts[:, 3:4]
                                     ).then_inc(sS.h, 1)
                        s.wait_ge(sS.h, bS_ + 2)
                        s.activation(rstdc, lvc, AF.Exp, scale=-0.5
                                     ).then_inc(sS.h, 1)
                bV2, bS2 = bV + 4, bS_ + 3
                with nc.Block() as blk:
                    @blk.vector
                    def _(v):
                        v.wait_ge(sS.h, bS2)
                        for u in range(8):
                            tt2, h = u // 4, u % 4
                            v.tensor_scalar(fscA[:, 0:256],
                                            gnb[:, tt2, h * 256:(h + 1) * 256],
                                            muc[:, u:u + 1], rstdc[:, u:u + 1],
                                            op0=ALU.subtract, op1=ALU.mult
                                            ).then_inc(sV.h, 1)
                            v.wait_ge(sV.h, bV2 + 3 * u + 1)
                            v.tensor_mul(fscA[:, 256:512], fscA[:, 0:256],
                                         gnwr[:, h * 256:(h + 1) * 256]
                                         ).then_inc(sV.h, 1)
                            v.wait_ge(sV.h, bV2 + 3 * u + 2)
                            v.tensor_add(x_res[:, 2 * ci + tt2,
                                               h * 256:(h + 1) * 256],
                                         x_res[:, 2 * ci + tt2,
                                               h * 256:(h + 1) * 256],
                                         fscA[:, 256:512]).then_inc(sV.h, 1)
                sV.n = bV2 + 24; sS.n = bS2
                BAR()

            bL = sL.n
            with nc.Block() as blk:
                @blk.gpsimd
                def _(g):
                    for kt in range(8):
                        g.dma_start(out=fscBb[:, 0:2816],
                                    in_=ff_up_h.ap()[l, kt * 128:(kt + 1) * 128, :]
                                    ).then_inc(sL.h, 16)
                        g.wait_ge(sL.h, bL + 32 * kt + 16)
                        g.dma_start(out=ffu_hbm.ap()[kt * 128:(kt + 1) * 128, :],
                                    in_=fscBb[:, 0:2816]).then_inc(sL.h, 16)
                        g.wait_ge(sL.h, bL + 32 * kt + 32)
                    for jt in range(11):
                        o0 = bL + 256 + 32 * jt
                        g.dma_start(out=fscBb[:, 0:1024],
                                    in_=ff_down_h.ap()[l, jt * 128:(jt + 1) * 128, :]
                                    ).then_inc(sL.h, 16)
                        g.wait_ge(sL.h, o0 + 16)
                        g.dma_start(out=ffd_hbm.ap()[jt * 128:(jt + 1) * 128, :],
                                    in_=fscBb[:, 0:1024]).then_inc(sL.h, 16)
                        g.wait_ge(sL.h, o0 + 32)
            sL.n = bL + 608
            BAR()
            emit_ln(x_res, apply_mode="copy", dst3=ytok)
            emit_t2d(ytok, xnT)
            bS_ = sS.n
            with nc.Block() as blk:
                @blk.scalar
                def _(s):
                    for dt in range(NDT):
                        inst = s.activation(xnT[:, dt, :], xnT[:, dt, :],
                                            AF.Identity,
                                            scale=lnw_sb[:, 1, l, dt:dt + 1])
                    inst.then_inc(sS.h, 1)
            sS.n = bS_ + 1
            BAR()
            wgj = arena[:, 8192:9216].rearrange("p (k c) -> p k c", k=8)
            wvj = arena[:, 9216:10240].rearrange("p (k c) -> p k c", k=8)
            wdj = arena[:, 10240:11264]
            ggj = arena[:, 11264:11520]
            prodj = arena[:, 11520:11776]
            for ci in range(8):
                t0 = ci * 256
                bL, bM, bS_, bV = sL.n, sM.n, sS.n, sV.n
                with nc.Block() as blk:
                    @blk.gpsimd
                    def _(g):
                        for j in range(11):
                            if j >= 1:
                                g.wait_ge(sM.h, bM + 3 * j)
                            g.dma_start(out=wgj[:, :, :],
                                        in_=ffu_hbm.ap()[:, j * 128:(j + 1) * 128]
                                        .rearrange("(k p) c -> p k c", p=128)
                                        ).then_inc(sL.h, 16)
                            g.dma_start(out=wvj[:, :, :],
                                        in_=ffu_hbm.ap()[:, UPP + j * 128:
                                                         UPP + (j + 1) * 128]
                                        .rearrange("(k p) c -> p k c", p=128)
                                        ).then_inc(sL.h, 16)
                            g.dma_start(out=wdj[:, :],
                                        in_=ffd_hbm.ap()[j * 128:(j + 1) * 128, :]
                                        ).then_inc(sL.h, 16)
                    @blk.tensor
                    def _(t):
                        for bank in range(4):
                            t.matmul(psB[:, bank * 512:(bank + 1) * 512], zeros128,
                                     xnT[:, 0, 0:512], start=True, stop=False,
                                     skip_group_check=True)
                        for j in range(11):
                            t.wait_ge(sL.h, bL + 48 * (j + 1))
                            if j >= 1:
                                t.wait_ge(sS.h, bS_ + j)
                                t.wait_ge(sV.h, bV + j)
                            for kt in range(8):
                                t.matmul(psA[:, 0:256], wgj[:, kt, :],
                                         xnT[:, kt, t0:t0 + 256],
                                         start=(kt == 0), stop=(kt == 7))
                            for kt in range(8):
                                inst = t.matmul(psA[:, 512:768], wvj[:, kt, :],
                                                xnT[:, kt, t0:t0 + 256],
                                                start=(kt == 0), stop=(kt == 7))
                            inst.then_inc(sM.h, 1)
                            t.wait_ge(sV.h, bV + j + 1)
                            for mt in range(8):
                                inst = t.matmul(psB[:, mt * 256:(mt + 1) * 256],
                                                wdj[:, mt * 128:(mt + 1) * 128],
                                                prodj[:, 0:256],
                                                start=False, stop=(j == 10),
                                                skip_group_check=True)
                            inst.then_inc(sM.h, 2)
                    @blk.scalar
                    def _(s):
                        for j in range(11):
                            s.wait_ge(sM.h, bM + 3 * j + 1)
                            s.activation(ggj[:, 0:256], psA[:, 0:256], AF.Gelu
                                         ).then_inc(sS.h, 1)
                        s.wait_ge(sM.h, bM + 33)
                        for mt in range(8):
                            s.activation(xnT[:, mt, t0:t0 + 256],
                                         psB[:, mt * 256:(mt + 1) * 256],
                                         AF.Identity).then_inc(sS.h, 1)
                    @blk.vector
                    def _(v):
                        for j in range(11):
                            v.wait_ge(sS.h, bS_ + j + 1)
                            v.tensor_mul(prodj[:, 0:256], ggj[:, 0:256],
                                         psA[:, 512:768]).then_inc(sV.h, 1)
                sL.n = bL + 528; sM.n = bM + 33; sS.n = bS_ + 19; sV.n = bV + 11
                BAR()
            emit_d2t(xnT, ytok)
            bV = sV.n
            with nc.Block() as blk:
                @blk.vector
                def _(v):
                    for tt in range(NTT):
                        inst = v.tensor_add(x_res[:, tt, :], x_res[:, tt, :],
                                            ytok[:, tt, :])
                    inst.then_inc(sV.h, 1)
            sV.n = bV + 1
            BAR()

        pw_rep = gnwr
        bL = sL.n
        with nc.Block() as blk:
            @blk.gpsimd
            def _(g):
                g.dma_start(out=pw_rep[:, :],
                            in_=post_wh.ap().to_broadcast([128, D])
                            ).then_inc(sL.h, 16)
                g.wait_ge(sL.h, bL + 16)
        sL.n = bL + 16
        BAR()
        emit_ln(x_res, apply_mode="inplace")
        bV = sV.n
        with nc.Block() as blk:
            @blk.vector
            def _(v):
                for tt in range(NTT):
                    inst = v.tensor_mul(x_res[:, tt, :], x_res[:, tt, :],
                                        pw_rep[:, :])
                inst.then_inc(sV.h, 1)
        sV.n = bV + 1
        BAR()
        emit_ln(x_res, apply_mode="copy", dst3=ytok)
        bL, bV = sL.n, sV.n
        with nc.Block() as blk:
            @blk.gpsimd
            def _(g):
                for tt in range(NTT):
                    if tt >= 2:
                        g.wait_ge(sV.h, bV + tt - 1)
                    g.dma_start(out=fscA[:, 0:D] if tt % 2 == 0 else fscB[:, 0:D],
                                in_=xresv_hbm.ap()[:, tt * D:(tt + 1) * D]
                                ).then_inc(sL.h, 16)
            @blk.vector
            def _(v):
                for tt in range(NTT):
                    v.wait_ge(sL.h, bL + 16 * (tt + 1))
                    xs = fscA[:, 0:D] if tt % 2 == 0 else fscB[:, 0:D]
                    v.tensor_add(x_res[:, tt, :], xs, ytok[:, tt, :]
                                 ).then_inc(sV.h, 1)
        sL.n = bL + 16 * NTT; sV.n = bV + NTT
        BAR()

    def emit_store():
        with nc.Block() as blk:
            @blk.scalar
            def _(s):
                s.dma_start(out=out_hbm.ap().rearrange("(a p) d -> p a d", p=128),
                            in_=x_res[:]).then_inc(sO.h, 16)
                s.wait_ge(sO.h, sO.n + 16)
        sO.n += 16

    emit_l0_attn()
    if upto != "l0attn":
        emit_l0_ffn()
    if upto in ("l1attn", "all"):
        emit_l1_attn()
    if upto in ("l1ffn", "all", "skipl1a"):
        emit_l1_ffn()
    if upto == "qdbg":
        emit_qdbg()
    emit_store()
    st.close()
    return nc


def kernel(**inputs):
    prep = _prep(inputs)
    x_full = np.ascontiguousarray(np.asarray(inputs["L0_attn_x"], np.float32))
    nc = build_program(_UPTO)
    in_maps = []
    for c in range(8):
        m = dict(prep)
        m["x"] = np.ascontiguousarray(x_full[c])
        in_maps.append(m)
    res = run_bass_kernel_spmd(nc, in_maps, core_ids=list(range(8)))
    return np.stack([r["out"] for r in res.results]).astype(np.float32)


# version 02 (final)
# speedup vs baseline: 1.1779x; 1.1779x over previous
import contextlib
import numpy as np
import concourse.bass as bass
import concourse.mybir as mybir
from concourse.bass_utils import run_bass_kernel_spmd

F32 = mybir.dt.float32
BF16 = mybir.dt.bfloat16
AF = mybir.ActivationFunctionType
ALU = mybir.AluOpType
AX = mybir.AxisListType.X

T = 2048
D = 1024
NTT = T // 128
NDT = D // 128
UP = 1344
UPP = 1408
E_MOE = 32
EPS = 1e-5
MAGIC = float(np.float32(3 << 22))
TWO_PI = float(2 * np.pi)

RIEMANN = (14.134725142, 21.022039639, 25.01085758, 30.424876126, 32.935061588,
           37.586178159, 40.918719012, 43.327073281, 48.005150881, 49.773832478,
           52.970321478, 56.446247697, 59.347044003, 60.831778525, 65.112544048,
           67.079810529, 69.546401711, 72.067157674, 75.704690699, 77.144840069,
           79.33737502, 82.910380854, 84.735492981, 87.425274613, 88.809111208,
           92.491899271, 94.651344041, 95.870634228, 98.831194218, 101.317851006,
           103.72553804, 105.446623052)

_UPTO = "all"


def _perm_qk():
    perm = np.zeros(1024, np.int64)
    for P in range(8):
        b = P * 128
        h0, h1 = 2 * P, 2 * P + 1
        perm[b +  0:b + 32] = h0 * 64 + np.arange(32) * 2
        perm[b + 32:b + 64] = h1 * 64 + np.arange(32) * 2
        perm[b + 64:b + 96] = h0 * 64 + np.arange(32) * 2 + 1
        perm[b + 96:b +128] = h1 * 64 + np.arange(32) * 2 + 1
    return perm


def _prep(inp):
    C = np.ascontiguousarray
    A = lambda k: np.asarray(inp[k], np.float32)
    o = {}
    pq = _perm_qk()
    wq = A("L0_attn_w_qkv")
    o["w_qkv"] = C(np.concatenate([wq[:, 0:1024][:, pq], wq[:, 1024:2048][:, pq],
                                   wq[:, 2048:3072]], axis=1))
    o["w_o"] = C(A("L0_attn_w_o"))
    o["log_scale"] = C(A("L0_attn_log_scale").reshape(16, 1))
    o["w1T"] = C(A("L0_ffn_W1").T)
    o["b1"] = C(A("L0_ffn_b1").reshape(32, 128).T)
    o["w2T"] = C(A("L0_ffn_W2").T)
    o["b2"] = C(A("L0_ffn_b2").reshape(8, 128).T)
    o["ln1_w"] = C(A("L1_attn_ln1_w").reshape(4, 8, 128).transpose(2, 0, 1))
    o["ln2_w"] = C(A("L1_attn_ln2_w").reshape(4, 8, 128).transpose(2, 0, 1))
    o["gn_w"] = C(A("L1_attn_gn_w"))
    o["post_w"] = C(A("L1_attn_post_w").reshape(1, 1024))
    o["conv_w"] = C(A("L1_attn_conv_w").reshape(4, 8, 128, 4).transpose(2, 0, 1, 3))
    o["conv_b"] = C(A("L1_attn_conv_b").reshape(4, 8, 128).transpose(2, 0, 1))
    for nm in ("wi", "wf", "wz", "wo"):
        sc = 0.5 if nm in ("wf", "wo") else 1.0
        o[nm + "T"] = C(A(f"L1_attn_{nm}").transpose(0, 1, 3, 2) * sc)
    rk_ = A("L1_attn_rker").reshape(4, 4, 256, 1024).copy()
    rk_[:, :, :, 256:512] *= 0.5
    rk_[:, :, :, 768:1024] *= 0.5
    o["rker"] = C(rk_)
    cb = A("L1_attn_cell_b").reshape(4, 4, 4, 2, 128)
    cbt = cb.transpose(4, 0, 2, 1, 3).reshape(128, 4, 32).copy()
    cbt[:, :, 8:16] *= 0.5
    cbt[:, :, 24:32] *= 0.5
    o["cell_b"] = C(cbt)
    fu = A("L1_attn_ff_up")
    fup = np.zeros((4, 1024, 2 * UPP), np.float32)
    fup[:, :, :UP] = fu[:, :, :UP]
    fup[:, :, UPP:UPP + UP] = fu[:, :, UP:]
    o["ff_up"] = C(fup)
    fd = A("L1_attn_ff_down")
    fdp = np.zeros((4, UPP, 1024), np.float32)
    fdp[:, :UP, :] = fd
    o["ff_down"] = C(fdp)
    o["router_wT"] = C(A("L1_ffn_router_w").T)
    o["router_b"] = C(A("L1_ffn_router_bias").reshape(32, 1))
    o["gate_wT"] = C(A("L1_ffn_gate_w").transpose(0, 2, 1))
    o["up_wT"] = C(A("L1_ffn_up_w").transpose(0, 2, 1))
    o["down_wT"] = C(A("L1_ffn_down_w").transpose(0, 2, 1))
    o["sgate_wT"] = C(A("L1_ffn_sgate_w").T)
    o["sup_wT"] = C(A("L1_ffn_sup_w").T)
    o["sdown_wT"] = C(A("L1_ffn_sdown_w").T)
    g = np.asarray(RIEMANN[:32], np.float32)
    invf = (1.0 / (g / g[0])).astype(np.float32)
    o["c_invf"] = C((np.tile(invf, 4) / np.float32(2 * np.pi)).reshape(128, 1))
    return o



def build_program(upto="all"):
    st = contextlib.ExitStack()
    nc = bass.Bass("TRN2", target_bir_lowering=False,
                   detect_race_conditions=False)

    din = lambda name, shape: nc.dram_tensor(name, shape, F32, kind="ExternalInput")

    x_hbm = din("x", [T, D])
    w_qkv_h = din("w_qkv", [D, 3 * D])
    w_o_h = din("w_o", [D, D])
    log_scale_h = din("log_scale", [16, 1])
    w1T_h = din("w1T", [D, 4096]); b1_h = din("b1", [128, 32])
    w2T_h = din("w2T", [4096, D]); b2_h = din("b2", [128, 8])
    ln1_wh = din("ln1_w", [128, 4, 8]); ln2_wh = din("ln2_w", [128, 4, 8])
    gn_wh = din("gn_w", [4, D]); post_wh = din("post_w", [1, D])
    conv_wh = din("conv_w", [128, 4, 8, 4]); conv_bh = din("conv_b", [128, 4, 8])
    wT_h = {nm: din(nm + "T", [4, 4, 256, 256]) for nm in ("wi", "wf", "wz", "wo")}
    rker_h = din("rker", [4, 4, 256, 1024])
    cell_bh = din("cell_b", [128, 4, 32])
    ff_up_h = din("ff_up", [4, D, 2 * UPP]); ff_down_h = din("ff_down", [4, UPP, D])
    router_wT_h = din("router_wT", [D, 32]); router_b_h = din("router_b", [32, 1])
    gate_wT_h = din("gate_wT", [E_MOE, D, 128]); up_wT_h = din("up_wT", [E_MOE, D, 128])
    down_wT_h = din("down_wT", [E_MOE, 128, D])
    sgate_wT_h = din("sgate_wT", [D, 512]); sup_wT_h = din("sup_wT", [D, 512])
    sdown_wT_h = din("sdown_wT", [512, D])
    c_invf_h = din("c_invf", [128, 1])

    out_hbm = nc.dram_tensor("out", [T, D], F32, kind="ExternalOutput")
    wq1_hbm = nc.dram_tensor("wq1i", [D, 4096], BF16, kind="Internal")
    wq2_hbm = nc.dram_tensor("wq2i", [4096, D], BF16, kind="Internal")
    xresv_hbm = nc.dram_tensor("xresvi", [128, NTT * D], F32, kind="Internal")
    ffu_hbm = nc.dram_tensor("ffui", [D, 2 * UPP], BF16, kind="Internal")
    ffd_hbm = nc.dram_tensor("ffdi", [UPP, D], BF16, kind="Internal")

    sbt = lambda n, s, d: st.enter_context(nc.sbuf_tensor(n, s, d))
    psbt = lambda n, s: st.enter_context(nc.psum_tensor(n, s, F32))

    x_res = sbt("x_res", [128, NTT, D], F32)
    xnT = sbt("xnT", [128, NDT, T], BF16)
    tokA = sbt("tokA", [128, 16384], BF16)
    arena = sbt("arena", [128, 16384], BF16)
    fscA = sbt("fscA", [128, 2048], F32)
    fscB = sbt("fscB", [128, 2048], F32)
    smalls = sbt("smalls", [128, 128], F32)
    consts = sbt("consts", [128, 64], F32)
    cbf = sbt("cbf", [128, 520], BF16)
    masks = sbt("masks", [128, 4, 512], BF16)
    wsm = sbt("wsm", [128, 2, 512], BF16)
    hbuf = sbt("hbuf", [128, 8], BF16)
    sst = sbt("sst", [128, 16], F32)
    sscr = sbt("sscr", [128, 128], F32)
    qx1 = sbt("qx1", [64, T], BF16); qx2 = sbt("qx2", [64, T], BF16)
    kx1 = sbt("kx1", [64, T], BF16); kx2 = sbt("kx2", [64, T], BF16)
    ytok = tokA.rearrange("p (a d) -> p a d", a=NTT)
    junkf = arena.bitcast(F32)[0:64, 5120:7168]
    attnT = tokA.rearrange("p (a d) -> p a d", a=NDT)
    junk = arena[:, 15360:16384]

    psA = psbt("psA", [128, 1024])
    psZ = psbt("psZ", [128, 512])
    psC = psbt("psC", [128, 512])
    psB = psbt("psB", [128, 2048])

    class Sem:
        def __init__(self, name):
            self.h = st.enter_context(nc.semaphore(name))
            self.n = 0
    sL, sT, sM, sV, sS, sG, sO = (Sem(x) for x in
                                  ("ld", "tr", "mm", "vv", "ss", "gg", "sto"))
    sMs, sVs, sSs = (Sem(x) for x in ("mms", "vvs", "sss"))
    BAR = nc.all_engine_barrier
    ts = bass.ts

    scan_regs_holder = []

    @contextlib.contextmanager
    def ForiScan(start, end, engines):
        if not scan_regs_holder:
            scan_regs_holder.append(nc.alloc_registers("scan_i", engines=engines))
        regs = scan_regs_holder[0]
        loop_id = nc.next_id()
        loop_start = f"scan_fori_{loop_id}_loop"
        loop_end = f"scan_fori_{loop_id}_end"
        nc.regs_mov(regs, start)
        nc.br(loop_start, engines=engines)
        box = []
        with nc.body(loop_start, valid_engines=engines):
            i = nc.snap(regs, min_val=start, max_val=end - 1)
            box.append(i)
            yield i
            nc.regs_alu(regs, regs, 1, op=ALU.add)
            nc.br_lt(regs, end, on_true=loop_start, on_false=loop_end,
                     engines=engines)
        nc.switch_bb(loop_end)
        for h in box[0].val.handles:
            nc.free_register(h)

    ENG3 = [mybir.EngineType.PE, mybir.EngineType.Activation, mybir.EngineType.DVE]
    slp = nc.alloc_registers("slp_i", engines=ENG3)
    r_vT = nc.alloc_registers("r_vT", engines=[mybir.EngineType.PE])
    r_vS = nc.alloc_registers("r_vS", engines=[mybir.EngineType.Activation])
    r_vV = nc.alloc_registers("r_vV", engines=[mybir.EngineType.DVE])
    r_mV = nc.alloc_registers("r_mV", engines=[mybir.EngineType.DVE])
    r_sV = nc.alloc_registers("r_sV", engines=[mybir.EngineType.DVE])
    r_iw = nc.alloc_registers("r_iw", engines=[mybir.EngineType.DVE])
    r_ih = nc.alloc_registers("r_ih", engines=[mybir.EngineType.DVE])
    iv_w = bass.RuntimeValue(r_iw, min_val=0, max_val=255)
    iv_h = bass.RuntimeValue(r_ih, min_val=0, max_val=255)

    def free_tmps(id0, id1):
        for k in range(id0, id1 + 1):
            for eng, enm in (("DVE", mybir.EngineType.DVE),
                             ("PE", mybir.EngineType.PE),
                             ("Activation", mybir.EngineType.Activation)):
                try:
                    nc.free_register(bass.RegisterHandle(name=f"{eng}_tmp_{k}",
                                                         engine=enm))
                except Exception:
                    pass

    with nc.Block() as blk:
        @blk.gpsimd
        def _(g):
            g.dma_start(out=x_res[:],
                        in_=x_hbm.ap().rearrange("(a p) d -> p a d", p=128)
                        ).then_inc(sL.h, 16)
            g.dma_start(out=consts[:, 1:2], in_=c_invf_h.ap()).then_inc(sL.h, 16)
            g.memset(consts[:, 0:1], 1.0)
            g.memset(consts[:, 3:4], 1e-5)
            g.memset(consts[0:1, 8:9], 1.0)
            g.memset(cbf[:, 0:1], 1.0)
            g.memset(cbf[0:1, 1:65], 1.0)
            g.memset(cbf[0:1, 192:320] if False else cbf[0:1, 1:129], 1.0)
            g.memset(cbf[:, 260:388], 1.0)
            g.memset(cbf[:, 392:520], 0.0)
            for j in range(4):
                inst = g.iota(fscA[:, j * 512:(j + 1) * 512], [[1, 512]],
                              base=-128 * j, channel_multiplier=-1,
                              allow_small_or_imprecise_dtypes=True)
                inst.then_inc(sG.h, 1)
            inst = g.iota(fscB[:, 0:128], [[1, 128]], base=0, channel_multiplier=-1,
                          allow_small_or_imprecise_dtypes=True)
            inst.then_inc(sG.h, 1)
        @blk.vector
        def _(v):
            v.wait_ge(sL.h, 32)
            v.wait_ge(sG.h, 5)
            for j in range(4):
                v.tensor_scalar(masks[:, j, :], fscA[:, j * 512:(j + 1) * 512],
                                -0.5, None, op0=ALU.is_ge)
            v.tensor_scalar(fscB[:, 128:256], fscB[:, 0:128], -0.5, None,
                            op0=ALU.is_ge)
            v.tensor_scalar(fscB[:, 0:128], fscB[:, 0:128], 0.5, None,
                            op0=ALU.is_le)
            v.tensor_mul(cbf[:, 130:258], fscB[:, 128:256], fscB[:, 0:128])
    sL.n = 32; sG.n = 5
    BAR()
    ones_col_bf = cbf[:, 0:1]
    ones_row64 = cbf[0:1, 1:65]
    ones_row128 = cbf[0:1, 1:129]
    ident128 = cbf[:, 130:258]
    allones = cbf[:, 260:388]
    zeros128 = cbf[:, 392:520]

    def emit_ln(src3, n_tiles=NTT, apply_mode="copy", dst3=None):
        musum = smalls[:, 0:n_tiles]
        s2c = smalls[:, 16:16 + n_tiles]
        muc = smalls[:, 32:32 + n_tiles]
        varc = smalls[:, 48:48 + n_tiles]
        rstdc = smalls[:, 64:64 + n_tiles]
        lvc = smalls[:, 80:80 + n_tiles]
        bV, bS = sV.n, sS.n
        with nc.Block() as blk:
            @blk.vector
            def _(v):
                for tt in range(n_tiles):
                    inst = v.reduce_sum(musum[:, tt:tt + 1], src3[:, tt, 0:D], axis=AX)
                inst.then_inc(sV.h, 1)
            @blk.scalar
            def _(s):
                for tt in range(n_tiles):
                    inst = s.activation(junk[:, 0:D], src3[:, tt, 0:D], AF.Square,
                                        accum_out=s2c[:, tt:tt + 1])
                inst.then_inc(sS.h, 1)
        with nc.Block() as blk:
            @blk.vector
            def _(v):
                v.wait_ge(sV.h, bV + 1)
                v.wait_ge(sS.h, bS + 1)
                v.tensor_scalar_mul(muc, musum, 1.0 / D).then_inc(sV.h, 1)
                v.wait_ge(sV.h, bV + 2)
                v.tensor_mul(varc, muc, muc).then_inc(sV.h, 1)
                v.wait_ge(sV.h, bV + 3)
                v.scalar_tensor_tensor(varc, s2c, 1.0 / D, varc,
                                       op0=ALU.mult, op1=ALU.subtract).then_inc(sV.h, 1)
            @blk.scalar
            def _(s):
                s.wait_ge(sV.h, bV + 4)
                s.activation(lvc, varc, AF.Ln, bias=consts[:, 3:4]).then_inc(sS.h, 1)
                s.wait_ge(sS.h, bS + 2)
                s.activation(rstdc, lvc, AF.Exp, scale=-0.5).then_inc(sS.h, 1)
        bV2, bS2 = bV + 4, bS + 3
        with nc.Block() as blk:
            @blk.vector
            def _(v):
                v.wait_ge(sS.h, bS2)
                for tt in range(n_tiles):
                    if apply_mode == "copy":
                        inst = v.tensor_scalar(dst3[:, tt, 0:D], src3[:, tt, 0:D],
                                               muc[:, tt:tt + 1], rstdc[:, tt:tt + 1],
                                               op0=ALU.subtract, op1=ALU.mult)
                    elif apply_mode == "inplace":
                        inst = v.tensor_scalar(src3[:, tt, 0:D], src3[:, tt, 0:D],
                                               muc[:, tt:tt + 1], rstdc[:, tt:tt + 1],
                                               op0=ALU.subtract, op1=ALU.mult)
                    else:
                        v.tensor_scalar(fscA[:, 0:D], src3[:, tt, 0:D],
                                        muc[:, tt:tt + 1], rstdc[:, tt:tt + 1],
                                        op0=ALU.subtract, op1=ALU.mult)
                        inst = v.tensor_add(x_res[:, tt, :], x_res[:, tt, :],
                                            fscA[:, 0:D])
                inst.then_inc(sV.h, 1)
        sV.n = bV2 + 1
        sS.n = bS2
        BAR()

    def emit_t2d(src3, dstT, n_tt=NTT, n_dt=NDT, dst_col_off=0):
        bT = sT.n
        with nc.Block() as blk:
            @blk.sync
            def _(sy):
                for tt in range(n_tt):
                    sy.dma_start_transpose(
                        dstT[:, :, dst_col_off + tt * 128:dst_col_off + (tt + 1) * 128],
                        src3[:, tt, 0:n_dt * 128]).then_inc(sT.h, 16)
            @blk.vector
            def _(v):
                v.wait_ge(sT.h, bT + 16 * n_tt)
        sT.n = bT + 16 * n_tt
        BAR()

    def emit_d2t(srcT, dst3, n_dt=NDT, n_tt=NTT):
        bT = sT.n
        with nc.Block() as blk:
            @blk.sync
            def _(sy):
                for dt in range(n_dt):
                    sy.dma_start_transpose(
                        dst3[:, :, dt * 128:(dt + 1) * 128],
                        srcT[:, dt, 0:n_tt * 128]).then_inc(sT.h, 16)
            @blk.vector
            def _(v):
                v.wait_ge(sT.h, bT + 16 * n_dt)
        sT.n = bT + 16 * n_dt
        BAR()

    def emit_l0_attn():
        emit_ln(x_res, apply_mode="copy", dst3=ytok)
        emit_t2d(ytok, xnT)

        bL, bS_, bV_ = sL.n, sS.n, sV.n
        with nc.Block() as blk:
            @blk.gpsimd
            def _(g):
                for Pp in range(8):
                    g.dma_start(out=smalls[0:32, 96 + Pp:97 + Pp],
                                in_=log_scale_h.ap()[2 * Pp:2 * Pp + 1, :]
                                .to_broadcast([32, 1])).then_inc(sL.h, 16)
                    g.dma_start(out=smalls[32:64, 96 + Pp:97 + Pp],
                                in_=log_scale_h.ap()[2 * Pp + 1:2 * Pp + 2, :]
                                .to_broadcast([32, 1])).then_inc(sL.h, 16)
            @blk.scalar
            def _(s):
                s.wait_ge(sL.h, bL + 256)
                inst = s.activation(smalls[0:64, 96:104], smalls[0:64, 96:104], AF.Exp)
                inst.then_inc(sS.h, 1)
            @blk.vector
            def _(v):
                v.wait_ge(sS.h, bS_ + 1)
                for Pp in range(8):
                    inst = v.tensor_mul(smalls[0:64, 112 + Pp:113 + Pp],
                                        smalls[0:64, 96 + Pp:97 + Pp],
                                        consts[0:64, 1:2])
                inst.then_inc(sV.h, 1)
        sL.n = bL + 256; sS.n = bS_ + 1; sV.n = bV_ + 1
        BAR()

        wqkv_sb = arena[:, 0:6144].rearrange("p (b s k c) -> p b s k c", b=2, s=3, k=8)
        ropeP = arena[0:64, 6144:10240].rearrange("p (a t) -> p a t", a=2)
        vtokbuf = arena[:, 10240:12288].rearrange("p (k c) -> p k c", k=16)
        vstage = arena[:, 12288:14336]
        uevac = arena[0:64, 12288:14336]
        expS = arena[:, 14336:14848]
        rzrow = arena[0:1, 14848:15360]
        attnT_e = attnT
        attnT_o_full = attnT
        cs_t = ropeP[:, 0, :]; sn_t = ropeP[:, 1, :]

        for Pp in range(8):
            wq_v = wqkv_sb[:, Pp % 2, :, :, :]
            bL = sL.n
            with nc.Block() as blk:
                @blk.gpsimd
                def _(g):
                    if Pp >= 2:
                        g.wait_ge(sM.h, sM.n)
                    for si, base in enumerate((Pp * 128, 1024 + Pp * 128,
                                               2048 + Pp * 128)):
                        g.dma_start(out=wq_v[:, si, :, :],
                                    in_=w_qkv_h.ap()[:, base:base + 128]
                                    .rearrange("(k p) c -> p k c", p=128)
                                    ).then_inc(sL.h, 16)
            sL.n = bL + 48

            coef = smalls[0:64, 112 + Pp:113 + Pp]
            bV_, bS_, bG_ = sV.n, sS.n, sG.n
            with nc.Block() as blk:
                @blk.gpsimd
                def _(g):
                    if Pp:
                        g.wait_ge(sV.h, bV_)
                    g.iota(fscA[0:64, 0:T], [[1, T]], base=0, channel_multiplier=0,
                           allow_small_or_imprecise_dtypes=True).then_inc(sG.h, 1)
                @blk.vector
                def _(v):
                    if Pp:
                        v.wait_ge(sS.h, bS_)
                    v.wait_ge(sG.h, bG_ + 1)
                    fr = fscA[0:64, 0:T]
                    fs = fscB[0:64, 0:T]
                    v.tensor_scalar_mul(fr, fr, coef)
                    v.tensor_scalar(fs, fr, MAGIC, -MAGIC, op0=ALU.add, op1=ALU.add)
                    v.tensor_sub(fs, fr, fs).then_inc(sV.h, 1)
                    v.tensor_scalar_add(fr, fr, 0.25)
                    v.wait_ge(sS.h, bS_ + 1)
                    v.tensor_scalar(fs, fr, MAGIC, -MAGIC, op0=ALU.add, op1=ALU.add)
                    v.tensor_sub(fs, fr, fs).then_inc(sV.h, 1)
                @blk.scalar
                def _(s):
                    s.wait_ge(sV.h, bV_ + 1)
                    s.activation(sn_t, fscB[0:64, 0:T], AF.Sin, scale=TWO_PI
                                 ).then_inc(sS.h, 1)
                    s.wait_ge(sV.h, bV_ + 2)
                    s.activation(cs_t, fscB[0:64, 0:T], AF.Sin, scale=TWO_PI
                                 ).then_inc(sS.h, 1)
            sV.n = bV_ + 2; sS.n = bS_ + 2; sG.n = bG_ + 1
            BAR()

            for si in range(3):
                bM, bV_, bS_ = sM.n, sV.n, sS.n
                with nc.Block() as blk:
                    if si < 2:
                        d1, d2 = (qx1, qx2) if si == 0 else (kx1, kx2)
                        @blk.tensor
                        def _(t):
                            t.wait_ge(sL.h, bL + 48)
                            for half in range(2):
                                if half == 1:
                                    t.wait_ge(sV.h, bV_ + 1)
                                for tq in range(4):
                                    for kt in range(8):
                                        inst = t.matmul(
                                            psB[0:64, tq * 512:(tq + 1) * 512],
                                            wq_v[:, si, kt, half * 64:half * 64 + 64],
                                            xnT[:, kt, tq * 512:(tq + 1) * 512],
                                            start=(kt == 0), stop=(kt == 7))
                                inst.then_inc(sM.h, 1)
                        @blk.vector
                        def _(v):
                            mcs = fscA[0:64, 0:T]
                            msn = fscB[0:64, 0:T]
                            v.wait_ge(sM.h, bM + 1)
                            v.tensor_mul(mcs, psB[0:64, 0:T], cs_t)
                            v.tensor_mul(msn, psB[0:64, 0:T], sn_t).then_inc(sV.h, 1)
                            v.wait_ge(sM.h, bM + 2)
                            v.tensor_mul(junkf[0:64, 0:T], psB[0:64, 0:T], sn_t)
                            v.tensor_sub(d1[:, :], mcs, junkf[0:64, 0:T])
                            v.tensor_mul(junkf[0:64, 0:T], psB[0:64, 0:T], cs_t)
                            v.tensor_add(d2[:, :], msn, junkf[0:64, 0:T]).then_inc(sV.h, 1)
                        sM.n = bM + 2
                        sV.n = bV_ + 2
                    else:
                        bT = sT.n
                        @blk.tensor
                        def _(t):
                            t.wait_ge(sL.h, bL + 48)
                            for tq in range(4):
                                for kt in range(8):
                                    inst = t.matmul(
                                        psB[:, tq * 512:(tq + 1) * 512],
                                        wq_v[:, 2, kt, :],
                                        xnT[:, kt, tq * 512:(tq + 1) * 512],
                                        start=(kt == 0), stop=(kt == 7))
                            inst.then_inc(sM.h, 1)
                        @blk.vector
                        def _(v):
                            v.wait_ge(sM.h, bM + 1)
                            v.tensor_copy(vstage[:, :], psB[:, 0:T]).then_inc(sV.h, 1)
                        @blk.sync
                        def _(sy):
                            sy.wait_ge(sV.h, bV_ + 1)
                            for hh in range(2):
                                sy.dma_start_transpose(
                                    vtokbuf[:, :, hh * 64:hh * 64 + 64],
                                    vstage[hh * 64:hh * 64 + 64, :]).then_inc(sT.h, 16)
                        sM.n = bM + 1
                        sV.n = bV_ + 1
                        sT.n = bT + 32
                BAR()

            for hh in range(2):
                rowlo, rowhi = (0, 64) if hh == 0 else (64, 128)
                r0 = hh * 32
                bM, bS_, bV_ = sM.n, sS.n, sV.n
                m_smm = {}; m_av = {}; m_bcast = {}
                s_exp = {}; s_rz = {}; s_uev = {}
                v_mask = {}; v_fin = {}
                mc = sc = vc = 0
                J = 0
                for qc in range(4):
                    for kt in range(4 * (qc + 1)):
                        mc += 1; m_smm[J] = mc
                        mc += 1; m_av[J] = mc
                        sc += 1; s_exp[J] = sc
                        if kt >= 4 * qc:
                            vc += 1; v_mask[J] = vc
                        J += 1
                    sc += 1; s_rz[qc] = sc
                    sc += 1; s_uev[qc] = sc
                    mc += 1; m_bcast[qc] = mc
                    vc += 1; v_fin[qc] = vc
                with nc.Block() as blk:
                    @blk.tensor
                    def _(t):
                        t.wait_ge(sT.h, sT.n)
                        J = 0
                        for qc in range(4):
                            ktmax = 4 * (qc + 1)
                            for kt in range(ktmax):
                                pss = psA[:, (J % 2) * 512:(J % 2) * 512 + 512]
                                if J >= 2:
                                    t.wait_ge(sS.h, bS_ + s_exp[J - 2])
                                t.matmul(pss,
                                         kx1[r0:r0 + 32, kt * 128:(kt + 1) * 128],
                                         qx1[r0:r0 + 32, qc * 512:(qc + 1) * 512],
                                         start=True, stop=False)
                                t.matmul(pss,
                                         kx2[r0:r0 + 32, kt * 128:(kt + 1) * 128],
                                         qx2[r0:r0 + 32, qc * 512:(qc + 1) * 512],
                                         start=False, stop=True).then_inc(sM.h, 1)
                                if J in v_mask:
                                    t.wait_ge(sV.h, bV_ + v_mask[J])
                                else:
                                    t.wait_ge(sS.h, bS_ + s_exp[J])
                                t.matmul(psB[:, qc * 512:(qc + 1) * 512],
                                         vtokbuf[:, kt, :],
                                         expS[:, :],
                                         start=(kt == 0), stop=(kt == ktmax - 1))
                                t.matmul(psZ[0:1, 0:512], ones_col_bf, expS[:, :],
                                         start=(kt == 0), stop=(kt == ktmax - 1)
                                         ).then_inc(sM.h, 1)
                                J += 1
                            t.wait_ge(sS.h, bS_ + s_rz[qc])
                            if qc:
                                t.wait_ge(sV.h, bV_ + v_fin[qc - 1])
                            t.matmul(psC[:, 0:512], ones_row128, rzrow[:, :],
                                     start=True, stop=True).then_inc(sM.h, 1)
                    @blk.scalar
                    def _(s):
                        J = 0
                        for qc in range(4):
                            ktmax = 4 * (qc + 1)
                            for kt in range(ktmax):
                                w = m_smm[J] if J == 0 else max(m_smm[J], m_av[J - 1])
                                s.wait_ge(sM.h, bM + w)
                                s.activation(expS[:, :],
                                             psA[:, (J % 2) * 512:(J % 2) * 512 + 512],
                                             AF.Exp, scale=0.125).then_inc(sS.h, 1)
                                J += 1
                            s.wait_ge(sM.h, bM + m_av[J - 1])
                            s.activation(fscA[0:1, 0:512], psZ[0:1, 0:512], AF.Ln)
                            if qc:
                                s.wait_ge(sM.h, bM + m_bcast[qc - 1])
                            s.activation(rzrow, fscA[0:1, 0:512], AF.Exp, scale=-1.0
                                         ).then_inc(sS.h, 1)
                            s.activation(
                                arena[rowlo:rowhi, 12288 + qc * 512:12288 + (qc + 1) * 512],
                                psB[rowlo:rowhi, qc * 512:(qc + 1) * 512],
                                AF.Identity).then_inc(sS.h, 1)
                    @blk.vector
                    def _(v):
                        J = 0
                        for qc in range(4):
                            ktmax = 4 * (qc + 1)
                            for kt in range(ktmax):
                                if J in v_mask:
                                    v.wait_ge(sS.h, bS_ + s_exp[J])
                                    v.tensor_mul(expS[:, :], expS[:, :],
                                                 masks[:, kt - 4 * qc, :]
                                                 ).then_inc(sV.h, 1)
                                J += 1
                            v.wait_ge(sM.h, bM + m_bcast[qc])
                            v.wait_ge(sS.h, bS_ + s_uev[qc])
                            v.tensor_mul(attnT[rowlo:rowhi, Pp, qc * 512:(qc + 1) * 512],
                                         arena[rowlo:rowhi,
                                               12288 + qc * 512:12288 + (qc + 1) * 512],
                                         psC[rowlo:rowhi, 0:512]).then_inc(sV.h, 1)
                sM.n = bM + mc
                sS.n = bS_ + sc
                sV.n = bV_ + vc
                BAR()
        BAR()

        wo_e = arena[0:64, 0:8192].rearrange("p (s c) -> p s c", s=8)
        wo_o = arena[64:128, 0:8192].rearrange("p (s c) -> p s c", s=8)
        wstage = arena[:, 8192:8704]
        bL = sL.n
        with nc.Block() as blk:
            @blk.gpsimd
            def _(g):
                for Pp2 in range(8):
                    g.dma_start(out=wo_e[:, Pp2, :],
                                in_=w_o_h.ap()[(2 * Pp2) * 64:(2 * Pp2 + 1) * 64, :]
                                ).then_inc(sL.h, 16)
                    g.dma_start(out=wo_o[:, Pp2, :],
                                in_=w_o_h.ap()[(2 * Pp2 + 1) * 64:(2 * Pp2 + 2) * 64, :]
                                ).then_inc(sL.h, 16)
        sL.n = bL + 256
        bM, bS_, bV_ = sM.n, sS.n, sV.n
        with nc.Block() as blk:
            @blk.tensor
            def _(t):
                t.wait_ge(sL.h, bL + 256)
                cnt = 0
                for tq in range(4):
                    for dmt in range(8):
                        pe = psA[:, (cnt % 2) * 512:(cnt % 2) * 512 + 512]
                        po = psZ[:, 0:512] if cnt % 2 == 0 else psC[:, 0:512]
                        if cnt >= 2:
                            t.wait_ge(sS.h, bS_ + cnt - 1)
                            t.wait_ge(sV.h, bV_ + cnt - 1)
                        for Pp2 in range(8):
                            inst = t.matmul(pe, wo_e[:, Pp2, dmt * 128:(dmt + 1) * 128],
                                            attnT[0:64, Pp2, tq * 512:(tq + 1) * 512],
                                            start=(Pp2 == 0), stop=(Pp2 == 7))
                        inst.then_inc(sM.h, 1)
                        for Pp2 in range(8):
                            inst = t.matmul(po, wo_o[:, Pp2, dmt * 128:(dmt + 1) * 128],
                                            attnT[64:128, Pp2, tq * 512:(tq + 1) * 512],
                                            start=(Pp2 == 0), stop=(Pp2 == 7))
                        inst.then_inc(sM.h, 1)
                        cnt += 1
            @blk.scalar
            def _(s):
                cnt = 0
                for tq in range(4):
                    for dmt in range(8):
                        pe = psA[:, (cnt % 2) * 512:(cnt % 2) * 512 + 512]
                        s.wait_ge(sM.h, bM + 2 * cnt + 1)
                        if cnt >= 1:
                            s.wait_ge(sV.h, bV_ + cnt)
                        s.activation(wstage, pe, AF.Identity).then_inc(sS.h, 1)
                        cnt += 1
            @blk.vector
            def _(v):
                cnt = 0
                for tq in range(4):
                    for dmt in range(8):
                        po = psZ[:, 0:512] if cnt % 2 == 0 else psC[:, 0:512]
                        v.wait_ge(sM.h, bM + 2 * cnt + 2)
                        v.wait_ge(sS.h, bS_ + cnt + 1)
                        v.tensor_add(xnT[:, dmt, tq * 512:(tq + 1) * 512],
                                     wstage, po).then_inc(sV.h, 1)
                        cnt += 1
        sM.n = bM + 64
        sS.n = bS_ + 32
        sV.n = bV_ + 32
        BAR()
        emit_d2t(xnT, ytok)
        emit_ln(ytok, apply_mode="resid")

    def emit_l0_ffn():
        emit_ln(x_res, apply_mode="copy", dst3=ytok)
        emit_t2d(ytok, xnT)
        arF = arena.bitcast(F32)
        w1v = lambda kt: w1T_h.ap()[(kt // 2) * 128:(kt // 2 + 1) * 128,
                                    (kt % 2) * 2048:(kt % 2 + 1) * 2048]
        wq1v = lambda kt: wq1_hbm.ap()[(kt // 2) * 128:(kt // 2 + 1) * 128,
                                       (kt % 2) * 2048:(kt % 2 + 1) * 2048]
        w2v = lambda kt: w2T_h.ap()[kt * 128:(kt + 1) * 128, :]
        wq2v = lambda kt: wq2_hbm.ap()[kt * 128:(kt + 1) * 128, :]
        abs1 = fscB[:, 0:16]
        abs2 = fscB[:, 16:48]

        def gamma_pass(wv, n_t, cols, absc):
            bL, bS_ = sL.n, sS.n
            with nc.Block() as blk:
                @blk.gpsimd
                def _(g):
                    for kt in range(n_t):
                        if kt >= 2:
                            g.wait_ge(sS.h, bS_ + kt - 1)
                        g.dma_start(out=arF[:, (kt % 2) * cols:(kt % 2) * cols + cols],
                                    in_=wv(kt)).then_inc(sL.h, 16)
                @blk.scalar
                def _(s):
                    for kt in range(n_t):
                        s.wait_ge(sL.h, bL + 16 * (kt + 1))
                        s.activation(fscA[:, 0:cols],
                                     arF[:, (kt % 2) * cols:(kt % 2) * cols + cols],
                                     AF.Abs, accum_out=absc[:, kt:kt + 1]
                                     ).then_inc(sS.h, 1)
            sL.n = bL + 16 * n_t; sS.n = bS_ + n_t
            BAR()
        gamma_pass(w1v, 16, 2048, abs1)
        gamma_pass(w2v, 32, 1024, abs2)

        srep1 = smalls[:, 126:127]
        srep2 = smalls[:, 127:128]
        bV, bS_, bM = sV.n, sS.n, sM.n
        with nc.Block() as blk:
            @blk.vector
            def _(v):
                v.reduce_sum(fscB[:, 60:61], abs1, axis=AX)
                v.reduce_sum(fscB[:, 62:63], abs2, axis=AX).then_inc(sV.h, 1)
        sV.n = bV + 1
        BAR()
        bV, bS_, bM = sV.n, sS.n, sM.n
        with nc.Block() as blk:
            @blk.tensor
            def _(t):
                t.matmul(psZ[0:1, 0:1], consts[:, 0:1], fscB[:, 60:61],
                         start=True, stop=True)
                t.matmul(psZ[0:1, 1:2], consts[:, 0:1], fscB[:, 62:63],
                         start=True, stop=True).then_inc(sM.h, 1)
            @blk.scalar
            def _(s):
                s.wait_ge(sM.h, bM + 1)
                s.activation(fscB[0:1, 65:67], psZ[0:1, 0:2], AF.Identity,
                             scale=1.0 / (1024.0 * 4096.0), bias=consts[0:1, 3:4]
                             ).then_inc(sS.h, 1)
            @blk.vector
            def _(v):
                v.wait_ge(sS.h, bS_ + 1)
                v.reciprocal(fscB[0:1, 67:69], fscB[0:1, 65:67]).then_inc(sV.h, 1)
        sM.n = bM + 1; sS.n = bS_ + 1; sV.n = bV + 1
        BAR()
        bV, bM = sV.n, sM.n
        with nc.Block() as blk:
            @blk.tensor
            def _(t):
                t.matmul(psZ[:, 4:6], consts[0:1, 8:9].to_broadcast([1, 128]),
                         fscB[0:1, 67:69], start=True, stop=True).then_inc(sM.h, 1)
            @blk.vector
            def _(v):
                v.wait_ge(sM.h, bM + 1)
                v.tensor_copy(srep1, psZ[:, 4:5])
                v.tensor_copy(srep2, psZ[:, 5:6]).then_inc(sV.h, 1)
        sM.n = bM + 1; sV.n = bV + 1
        BAR()

        def quant_pass(wv, n_t, cols, wqv, srep):
            bL, bV = sL.n, sV.n
            with nc.Block() as blk:
                @blk.gpsimd
                def _(g):
                    for kt in range(n_t):
                        if kt >= 2:
                            g.wait_ge(sV.h, bV + kt - 1)
                        g.dma_start(out=arF[:, (kt % 2) * cols:(kt % 2) * cols + cols],
                                    in_=wv(kt)).then_inc(sL.h, 16)
                        g.wait_ge(sV.h, bV + kt + 1)
                        g.dma_start(out=wqv(kt),
                                    in_=arena[:, 8192 + (kt % 2) * cols:
                                              8192 + (kt % 2) * cols + cols]
                                    ).then_inc(sL.h, 16)
                @blk.vector
                def _(v):
                    for kt in range(n_t):
                        v.wait_ge(sL.h, bL + 32 * kt + 16)
                        src_ = arF[:, (kt % 2) * cols:(kt % 2) * cols + cols]
                        v.tensor_scalar(fscA[:, 0:cols], src_, srep, 0.5,
                                        op0=ALU.mult, op1=ALU.is_ge)
                        v.tensor_scalar(fscB[:, 0:cols], src_, srep, -0.5,
                                        op0=ALU.mult, op1=ALU.is_le)
                        v.tensor_sub(arena[:, 8192 + (kt % 2) * cols:
                                           8192 + (kt % 2) * cols + cols],
                                     fscA[:, 0:cols], fscB[:, 0:cols]
                                     ).then_inc(sV.h, 1)
            with nc.Block() as blk2:
                @blk2.gpsimd
                def _(g):
                    g.wait_ge(sL.h, bL + 32 * n_t)
            sL.n = bL + 32 * n_t; sV.n = bV + n_t
            BAR()
        quant_pass(w1v, 16, 2048, wq1v, srep1)
        quant_pass(w2v, 32, 1024, wq2v, srep2)

        b1_sb = fscB[:, 0:32]
        b2_sb = fscB[:, 32:40]
        bL = sL.n
        with nc.Block() as blk:
            @blk.gpsimd
            def _(g):
                g.dma_start(out=b1_sb, in_=b1_h.ap()).then_inc(sL.h, 16)
                g.dma_start(out=b2_sb, in_=b2_h.ap()).then_inc(sL.h, 16)
            @blk.vector
            def _(v):
                v.wait_ge(sL.h, bL + 32)
        sL.n = bL + 32
        BAR()

        h_sb = arena[:, 0:8192].rearrange("p (a c) -> p a c", a=32)
        wq1s = arena[:, 8192:16384].rearrange("p (b k c) -> p b k c", b=2, k=8)
        tc = 256
        for tqi in range(8):
            t0 = tqi * tc
            bL, bM, bS_ = sL.n, sM.n, sS.n
            with nc.Block() as blk:
                @blk.gpsimd
                def _(g):
                    for fc in range(8):
                        if fc >= 1:
                            g.wait_ge(sM.h, bM + 4 * fc)
                        g.dma_start(out=wq1s[:, fc % 2, :, :],
                                    in_=wq1_hbm.ap()[:, fc * 512:(fc + 1) * 512]
                                    .rearrange("(k p) c -> p k c", p=128)
                                    ).then_inc(sL.h, 16)
                @blk.tensor
                def _(t):
                    for fc in range(8):
                        t.wait_ge(sL.h, bL + 16 * (fc + 1))
                        if fc >= 1:
                            t.wait_ge(sS.h, bS_ + 4 * fc)
                        for mt in range(4):
                            for kt in range(8):
                                inst = t.matmul(
                                    psB[:, mt * 512:mt * 512 + tc],
                                    wq1s[:, fc % 2, kt, mt * 128:(mt + 1) * 128],
                                    xnT[:, kt, t0:t0 + tc],
                                    start=(kt == 0), stop=(kt == 7))
                            inst.then_inc(sM.h, 1)
                @blk.scalar
                def _(s):
                    for fc in range(8):
                        for mt in range(4):
                            s.wait_ge(sM.h, bM + 4 * fc + mt + 1)
                            s.activation(h_sb[:, fc * 4 + mt, 0:tc],
                                         psB[:, mt * 512:mt * 512 + tc],
                                         AF.Relu,
                                         bias=b1_sb[:, fc * 4 + mt:fc * 4 + mt + 1]
                                         ).then_inc(sS.h, 1)
            sL.n = bL + 128; sM.n = bM + 32; sS.n = bS_ + 32
            BAR()
            bL, bM, bS_ = sL.n, sM.n, sS.n
            with nc.Block() as blk:
                @blk.gpsimd
                def _(g):
                    for dh in range(2):
                        for kt in range(32):
                            j = dh * 32 + kt
                            if j >= 1:
                                g.wait_ge(sM.h, bM + j)
                            g.dma_start(out=wsm[:, j % 2, 0:512],
                                        in_=wq2_hbm.ap()[kt * 128:(kt + 1) * 128,
                                                         dh * 512:(dh + 1) * 512]
                                        ).then_inc(sL.h, 16)
                @blk.tensor
                def _(t):
                    for dh in range(2):
                        if dh == 1:
                            t.wait_ge(sS.h, bS_ + 4)
                        for kt in range(32):
                            j = dh * 32 + kt
                            t.wait_ge(sL.h, bL + 16 * (j + 1))
                            for mt in range(4):
                                inst = t.matmul(
                                    psB[:, mt * 512:mt * 512 + tc],
                                    wsm[:, j % 2, mt * 128:(mt + 1) * 128],
                                    h_sb[:, kt, 0:tc],
                                    start=(kt == 0), stop=(kt == 31))
                            inst.then_inc(sM.h, 1)
                @blk.scalar
                def _(s):
                    for dh in range(2):
                        s.wait_ge(sM.h, bM + 32 * (dh + 1))
                        for mt in range(4):
                            dmt = dh * 4 + mt
                            s.activation(xnT[:, dmt, t0:t0 + tc],
                                         psB[:, mt * 512:mt * 512 + tc],
                                         AF.Identity, bias=b2_sb[:, dmt:dmt + 1]
                                         ).then_inc(sS.h, 1)
            sL.n = bL + 16 * 64; sM.n = bM + 64; sS.n = bS_ + 8
            BAR()
        emit_d2t(xnT, ytok)
        emit_ln(ytok, apply_mode="resid")

    def emit_qdbg():
        bL, bV = sL.n, sV.n
        with nc.Block() as blk:
            @blk.vector
            def _(v):
                for a in range(16):
                    v.tensor_copy(x_res[:, a, :],
                                  xnT[:, a // 2, (a % 2) * 1024:(a % 2 + 1) * 1024]
                                  ).then_inc(sV.h, 1)
        sV.n = bV + 16
        BAR()

    def emit_l1_ffn():
        emit_ln(x_res, apply_mode="copy", dst3=ytok)
        emit_t2d(ytok, xnT)
        rwT = arena[:, 0:256].rearrange("p (k c) -> p k c", k=8)
        wnTc = arena[0:64, 256:384]
        wG = arena[:, 768:1792].rearrange("p (k c) -> p k c", k=8)
        wU = arena[:, 1792:2816].rearrange("p (k c) -> p k c", k=8)
        wD = arena[:, 2816:3840]
        sgT = arena[:, 3840:7936].rearrange("p (k c) -> p k c", k=8)
        suT = arena[:, 7936:12032].rearrange("p (k c) -> p k c", k=8)
        sdT = arena[:, 12032:16128].rearrange("p (k c) -> p k c", k=4)
        rlog = qx1[0:32, :]
        wn_tok = tokA[:, 0:512].rearrange("p (a c) -> p a c", a=16)

        bL, bM, bS_ = sL.n, sM.n, sS.n
        with nc.Block() as blk:
            @blk.gpsimd
            def _(g):
                g.dma_start(out=rwT[:, :, :],
                            in_=router_wT_h.ap().rearrange("(k p) c -> p k c", p=128)
                            ).then_inc(sL.h, 16)
                g.dma_start(out=smalls[0:32, 8:9], in_=router_b_h.ap()).then_inc(sL.h, 16)
            @blk.tensor
            def _(t):
                t.wait_ge(sL.h, bL + 32)
                for tq in range(4):
                    inst = t.matmul(psB[0:32, tq * 512:(tq + 1) * 512], rwT[:, 0, :]
                                    if False else rwT[:, 0, :],
                                    xnT[:, 0, tq * 512:(tq + 1) * 512],
                                    start=True, stop=False)
                    for kt in range(1, 8):
                        inst = t.matmul(psB[0:32, tq * 512:(tq + 1) * 512],
                                        rwT[:, kt, :],
                                        xnT[:, kt, tq * 512:(tq + 1) * 512],
                                        start=False, stop=(kt == 7))
                inst.then_inc(sM.h, 1)
            @blk.scalar
            def _(s):
                s.wait_ge(sM.h, bM + 1)
                s.activation(rlog[:, :], psB[0:32, 0:T], AF.Identity,
                             bias=smalls[0:32, 8:9]).then_inc(sS.h, 1)
        sL.n = bL + 32; sM.n = bM + 1; sS.n = bS_ + 1
        BAR()
        bT = sT.n
        with nc.Block() as blk:
            @blk.sync
            def _(sy):
                sy.dma_start_transpose(wn_tok[:, :, 0:32], rlog[:, :]).then_inc(sT.h, 16)
            @blk.vector
            def _(v):
                v.wait_ge(sT.h, bT + 16)
        sT.n = bT + 16
        BAR()
        mxc = smalls[:, 0:1]; nmx = smalls[:, 1:2]; sc_ = smalls[:, 2:3]
        rs_ = smalls[:, 3:4]; v4 = smalls[:, 4:5]; ws_ = smalls[:, 5:6]
        rws = smalls[:, 6:7]
        pfull = fscA[:, 0:32]; m8 = fscA[:, 32:40]; wfull = fscA[:, 40:72]
        bV, bS_ = sV.n, sS.n
        with nc.Block() as blk:
            @blk.vector
            def _(v):
                for tt in range(NTT):
                    vb = sV.n + 10 * tt
                for tt in range(NTT):
                    base = bV + 7 * tt
                    v.reduce_max(mxc, wn_tok[:, tt, 0:32], axis=AX)
                    v.wait_ge(sV.h, 0)
                    v.tensor_scalar_mul(nmx, mxc, -1.0).then_inc(sV.h, 1)
                    v.wait_ge(sS.h, bS_ + 2 * tt + 1)
                    v.reciprocal(rs_, sc_).then_inc(sV.h, 1)
                    v.wait_ge(sV.h, base + 2)
                    v.tensor_scalar_mul(pfull, fscB[:, 0:32], rs_
                                        ).then_inc(sV.h, 1)
                    v.wait_ge(sV.h, base + 3)
                    v.max(out=m8, in_=pfull).then_inc(sV.h, 1)
                    v.wait_ge(sV.h, base + 4)
                    v.tensor_scalar(wfull, pfull, m8[:, 3:4], None,
                                    op0=ALU.is_ge).then_inc(sV.h, 1)
                    v.wait_ge(sV.h, base + 5)
                    v.tensor_mul(wfull, wfull, pfull)
                    v.reduce_sum(ws_, wfull, axis=AX).then_inc(sV.h, 1)
                    v.wait_ge(sV.h, base + 6)
                    v.tensor_scalar_max(ws_, ws_, 1e-12)
                    v.reciprocal(rws, ws_).then_inc(sV.h, 1)
                    v.wait_ge(sV.h, base + 7)
                    v.tensor_scalar_mul(wn_tok[:, tt, 0:32], wfull, rws)
            @blk.scalar
            def _(s):
                for tt in range(NTT):
                    base = bV + 7 * tt
                    s.wait_ge(sV.h, base + 1)
                    s.activation(fscB[:, 0:32], wn_tok[:, tt, 0:32], AF.Exp,
                                 bias=nmx, accum_out=sc_).then_inc(sS.h, 1)
                    s.wait_ge(sV.h, base + 3)
                    s.activation(junk[:, 0:1], sc_, AF.Identity).then_inc(sS.h, 1)
        sV.n = bV + 7 * NTT
        sS.n = bS_ + 2 * NTT
        BAR()

        bL = sL.n
        with nc.Block() as blk:
            @blk.gpsimd
            def _(g):
                g.dma_start(out=sgT[:, :, :],
                            in_=sgate_wT_h.ap().rearrange("(k p) c -> p k c", p=128)
                            ).then_inc(sL.h, 16)
                g.dma_start(out=suT[:, :, :],
                            in_=sup_wT_h.ap().rearrange("(k p) c -> p k c", p=128)
                            ).then_inc(sL.h, 16)
                g.dma_start(out=sdT[:, :, :],
                            in_=sdown_wT_h.ap().rearrange("(k p) c -> p k c", p=128)
                            ).then_inc(sL.h, 16)
        sL.n = bL + 48
        BAR()

        tc = 256
        sg_sb = tokA[:, 512:768]
        hu_sb = fscB[:, 0:256]
        hp_sb = tokA[:, 768:1024]
        for ci in range(8):
            t0 = ci * tc
            ttA, ttB = 2 * ci, 2 * ci + 1
            bM, bV = sM.n, sV.n
            with nc.Block() as blk:
                @blk.tensor
                def _(t):
                    psAb = psA.bitcast(BF16)
                    t.transpose(psAb[0:64, 0:128],
                                wn_tok[:, ttA, 0:32].rearrange("p c -> p c")
                                if False else
                                tokA[:, ttA * 32:ttA * 32 + 64],
                                ident128).then_inc(sM.h, 1)
                @blk.vector
                def _(v):
                    v.wait_ge(sM.h, bM + 1)
                    v.tensor_copy(wnTc, psA.bitcast(BF16)[0:64, 0:128]
                                  ).then_inc(sV.h, 1)
            sM.n = bM + 1; sV.n = bV + 1
            BAR()
            bL, bM, bS_, bV = sL.n, sM.n, sS.n, sV.n
            with nc.Block() as blk:
                @blk.gpsimd
                def _(g):
                    for e in range(E_MOE):
                        if e >= 1:
                            g.wait_ge(sM.h, bM + 3 * e)
                        g.dma_start(out=wG[:, :, :],
                                    in_=gate_wT_h.ap()[e].rearrange(
                                        "(k p) c -> p k c", p=128)).then_inc(sL.h, 16)
                        g.dma_start(out=wU[:, :, :],
                                    in_=up_wT_h.ap()[e].rearrange(
                                        "(k p) c -> p k c", p=128)).then_inc(sL.h, 16)
                        g.dma_start(out=wD[:, :],
                                    in_=down_wT_h.ap()[e]).then_inc(sL.h, 16)
                @blk.tensor
                def _(t):
                    for bank in range(4):
                        t.matmul(psB[:, bank * 512:(bank + 1) * 512], zeros128,
                                 xnT[:, 0, 0:512], start=True, stop=False,
                                 skip_group_check=True)
                    for e in range(E_MOE):
                        t.wait_ge(sL.h, bL + 48 * (e + 1))
                        if e >= 1:
                            t.wait_ge(sS.h, bS_ + e)
                            t.wait_ge(sV.h, bV + 3 * e)
                        for kt in range(8):
                            t.matmul(psA[:, 0:tc], wG[:, kt, :],
                                     xnT[:, kt, t0:t0 + tc],
                                     start=(kt == 0), stop=(kt == 7))
                        for kt in range(8):
                            inst = t.matmul(psA[:, 512:512 + tc], wU[:, kt, :],
                                            xnT[:, kt, t0:t0 + tc],
                                            start=(kt == 0), stop=(kt == 7))
                        inst.then_inc(sM.h, 1)
                        t.matmul(psC[:, 0:128],
                                 ident128[0:64, e:e + 1].to_broadcast([64, 128]),
                                 wnTc, start=True, stop=True)
                        inst = t.matmul(psC[:, 128:256],
                                        ident128[0:64, 32 + e:33 + e
                                                 ].to_broadcast([64, 128]),
                                        wnTc, start=True, stop=True)
                        inst.then_inc(sM.h, 1)
                        t.wait_ge(sV.h, bV + 3 * e + 3)
                        for mt in range(8):
                            inst = t.matmul(psB[:, mt * tc:(mt + 1) * tc],
                                            wD[:, mt * 128:(mt + 1) * 128],
                                            hp_sb[:, 0:tc],
                                            start=False, stop=False,
                                            skip_group_check=True)
                        inst.then_inc(sM.h, 1)
                    t.wait_ge(sS.h, bS_ + 32 + 0)
                    t.wait_ge(sV.h, bV + 96)
                    for st4 in range(4):
                        for kt in range(8):
                            inst = t.matmul(psA[:, st4 * tc:(st4 + 1) * tc]
                                            if st4 < 4 else None,
                                            sgT[:, kt, st4 * 128:(st4 + 1) * 128],
                                            xnT[:, kt, t0:t0 + tc],
                                            start=(kt == 0), stop=(kt == 7))
                    inst.then_inc(sM.h, 1)
                    for st4 in range(4):
                        ps_u = (psZ if st4 < 2 else psC)
                        for kt in range(8):
                            inst = t.matmul(ps_u[:, (st4 % 2) * tc:(st4 % 2 + 1) * tc],
                                            suT[:, kt, st4 * 128:(st4 + 1) * 128],
                                            xnT[:, kt, t0:t0 + tc],
                                            start=(kt == 0), stop=(kt == 7))
                    inst.then_inc(sM.h, 1)
                    for st4 in range(4):
                        t.wait_ge(sV.h, bV + 96 + 2 * (st4 + 1))
                        for mt in range(8):
                            inst = t.matmul(psB[:, mt * tc:(mt + 1) * tc],
                                            sdT[:, st4, mt * 128:(mt + 1) * 128],
                                            hp_sb[:, 0:tc],
                                            start=False, stop=(st4 == 3),
                                            skip_group_check=True)
                        inst.then_inc(sM.h, 1)
                @blk.scalar
                def _(s):
                    for e in range(E_MOE):
                        s.wait_ge(sM.h, bM + 3 * e + 1)
                        s.activation(sg_sb[:, 0:tc], psA[:, 0:tc], AF.Silu
                                     ).then_inc(sS.h, 1)
                    s.wait_ge(sM.h, bM + 96 + 1)
                    for st4 in range(4):
                        s.activation(sg_sb[:, 0:tc], psA[:, st4 * tc:(st4 + 1) * tc],
                                     AF.Silu).then_inc(sS.h, 1)
                        s.wait_ge(sV.h, bV + 96 + 2 * (st4 + 1))
                    s.wait_ge(sM.h, bM + 96 + 6)
                    for mt in range(8):
                        s.activation(xnT[:, mt, t0:t0 + tc],
                                     psB[:, mt * tc:(mt + 1) * tc], AF.Identity
                                     ).then_inc(sS.h, 1)
                @blk.vector
                def _(v):
                    for e in range(E_MOE):
                        vb = bV + 3 * e
                        v.wait_ge(sS.h, bS_ + e + 1)
                        v.tensor_mul(hu_sb, sg_sb[:, 0:tc], psA[:, 512:512 + tc]
                                     ).then_inc(sV.h, 1)
                        v.wait_ge(sV.h, vb + 1)
                        v.wait_ge(sM.h, bM + 3 * e + 2)
                        v.tensor_mul(fscA[:, 0:tc], hu_sb,
                                     psC[:, 0:tc]).then_inc(sV.h, 1)
                        v.wait_ge(sV.h, vb + 2)
                        v.tensor_copy(hp_sb[:, 0:tc], fscA[:, 0:tc]).then_inc(sV.h, 1)
                    v.wait_ge(sM.h, bM + 96 + 2)
                    for st4 in range(4):
                        vb = bV + 96 + 2 * st4
                        v.wait_ge(sS.h, bS_ + 32 + st4 + 1)
                        ps_u = (psZ if st4 < 2 else psC)
                        v.tensor_mul(fscA[:, 0:tc], sg_sb[:, 0:tc],
                                     ps_u[:, (st4 % 2) * tc:(st4 % 2 + 1) * tc]
                                     ).then_inc(sV.h, 1)
                        v.wait_ge(sV.h, vb + 1)
                        v.tensor_copy(hp_sb[:, 0:tc], fscA[:, 0:tc]).then_inc(sV.h, 1)
            sL.n = bL + 48 * 32
            sM.n = bM + 96 + 6
            sS.n = bS_ + 32 + 4 + 8
            sV.n = bV + 96 + 8
            BAR()
        emit_d2t(xnT, ytok)
        emit_ln(ytok, apply_mode="resid")

    def emit_l1_attn():
        fscBb = fscB.bitcast(BF16)
        hns = fscBb[:, 2048:4096].rearrange("p (a t) -> p a t", a=8)
        hv = hns.rearrange("p a t -> p t a")
        gnwr = masks[:, 0:2, :].rearrange("p a c -> p (a c)")
        mf = masks.bitcast(F32).rearrange("p a c -> p (a c)")
        cvw = mf[:, 512:640].rearrange("p (a b k) -> p a b k", a=4, b=8)
        cvb = mf[:, 640:672].rearrange("p (a b) -> p a b", a=4)
        lnw_sb = mf[:, 672:736].rearrange("p (a b c) -> p a b c", a=2, b=4)
        cbl = mf[:, 736:864].rearrange("p (a b) -> p a b", a=4)
        wsf = wsm.rearrange("p a c -> p (a c)")
        rkT = arena[:, 0:8192].rearrange("p (h k m c) -> p h k m c",
                                         h=4, k=2, m=8)
        wxb = arena[:, 8192:16384].rearrange("p (t c) -> p t c", t=256)
        hxT = tokA.rearrange("p (a t) -> p a t", a=NDT)
        EN = mybir.EngineType

        bL = sL.n
        with nc.Block() as blk:
            @blk.gpsimd
            def _(g):
                g.dma_start(out=xresv_hbm.ap(),
                            in_=x_res[:].rearrange("p a d -> p (a d)")
                            ).then_inc(sL.h, 16)
                g.dma_start(out=cvw[:], in_=conv_wh.ap()).then_inc(sL.h, 16)
                g.dma_start(out=cvb[:], in_=conv_bh.ap()).then_inc(sL.h, 16)
                g.dma_start(out=lnw_sb[:, 0, :, :], in_=ln1_wh.ap()).then_inc(sL.h, 16)
                g.dma_start(out=lnw_sb[:, 1, :, :], in_=ln2_wh.ap()).then_inc(sL.h, 16)
                g.dma_start(out=cbl[:], in_=cell_bh.ap()).then_inc(sL.h, 16)
                g.wait_ge(sL.h, bL + 96)
        sL.n = bL + 96
        BAR()
        emit_ln(x_res, apply_mode="inplace")

        for l in range(4):
            bL = sL.n
            with nc.Block() as blk:
                @blk.gpsimd
                def _(g):
                    g.dma_start(out=gnwr[:, :],
                                in_=gn_wh.ap()[l:l + 1, :].to_broadcast([128, D])
                                ).then_inc(sL.h, 16)
                    for h in range(4):
                        g.dma_start(out=rkT[:, h, :, :, :],
                                    in_=rker_h.ap()[l, h].rearrange(
                                        "(k p) (m c) -> p k m c", p=128, c=128)
                                    ).then_inc(sL.h, 16)
                    g.wait_ge(sL.h, bL + 80)
            sL.n = bL + 80
            BAR()

            emit_ln(x_res, apply_mode="copy", dst3=ytok)
            emit_t2d(ytok, xnT)
            bS_ = sS.n
            with nc.Block() as blk:
                @blk.scalar
                def _(s):
                    for dt in range(NDT):
                        inst = s.activation(xnT[:, dt, :], xnT[:, dt, :],
                                            AF.Identity,
                                            scale=lnw_sb[:, 0, l, dt:dt + 1])
                    inst.then_inc(sS.h, 1)
            sS.n = bS_ + 1
            BAR()

            bV, bS_, bG = sV.n, sS.n, sG.n
            with nc.Block() as blk:
                @blk.gpsimd
                def _(g):
                    g.memset(fscBb[:, 0:4], 0.0)
                    g.tensor_copy(consts[:, 0:1], consts[:, 0:1]).then_inc(sG.h, 1)
                @blk.vector
                def _(v):
                    v.wait_ge(sG.h, bG + 1)
                    for dt in range(NDT):
                        if dt:
                            v.wait_ge(sS.h, bS_ + dt)
                        v.tensor_copy(fscBb[:, 4:4 + T], xnT[:, dt, :])
                        v.tensor_scalar_mul(fscA[:, 0:T], fscBb[:, 1:1 + T],
                                            cvw[:, l, dt, 0:1])
                        for k in range(1, 4):
                            inst = v.scalar_tensor_tensor(
                                fscA[:, 0:T], fscBb[:, 1 + k:1 + k + T],
                                cvw[:, l, dt, k:k + 1], fscA[:, 0:T],
                                op0=ALU.mult, op1=ALU.add)
                        inst.then_inc(sV.h, 1)
                @blk.scalar
                def _(s):
                    for dt in range(NDT):
                        s.wait_ge(sV.h, bV + dt + 1)
                        s.activation(hxT[:, dt, :], fscA[:, 0:T], AF.Silu,
                                     bias=cvb[:, l, dt:dt + 1]).then_inc(sS.h, 1)
            sV.n = bV + NDT; sG.n = bG + 1; sS.n = bS_ + NDT
            BAR()

            bG = sG.n
            with nc.Block() as blk:
                @blk.gpsimd
                def _(g):
                    g.memset(sst[:], 0.0)
                    g.memset(hbuf[:], 0.0)
                    g.tensor_copy(consts[:, 0:1], consts[:, 0:1]).then_inc(sG.h, 1)
                @blk.vector
                def _(v):
                    v.wait_ge(sG.h, bG + 1)
            sG.n = bG + 1
            BAR()

            for ci in range(8):
                t0 = ci * 256
                bL, bM, bV = sL.n, sM.n, sV.n
                with nc.Block() as blk:
                    @blk.gpsimd
                    def _(g):
                        for h in range(4):
                            for gi_, wname in enumerate(("wi", "wf", "wz", "wo")):
                                idx = h * 4 + gi_
                                if idx >= 2:
                                    g.wait_ge(sM.h, bM + 2 * (idx - 1))
                                g.dma_start(
                                    out=wsf[:, (idx % 2) * 512:(idx % 2) * 512 + 512]
                                    .rearrange("p (k c) -> p k c", k=2),
                                    in_=wT_h[wname].ap()[l, h].rearrange(
                                        "(k p) c -> p k c", p=128)
                                    ).then_inc(sL.h, 16)
                    @blk.tensor
                    def _(t):
                        for h in range(4):
                            for gi_ in range(4):
                                idx = h * 4 + gi_
                                srcT = hxT if gi_ < 2 else xnT
                                t.wait_ge(sL.h, bL + 16 * (idx + 1))
                                if idx >= 2:
                                    t.wait_ge(sV.h, bV + 2 * (idx - 1))
                                pso = psA if idx % 2 == 0 else psB
                                wslot = (idx % 2) * 512
                                for ot in range(2):
                                    for kt in range(2):
                                        inst = t.matmul(
                                            pso[:, ot * 512:ot * 512 + 256],
                                            wsf[:, wslot + kt * 256 + ot * 128:
                                                wslot + kt * 256 + ot * 128 + 128],
                                            srcT[:, 2 * h + kt, t0:t0 + 256],
                                            start=(kt == 0), stop=(kt == 1))
                                    inst.then_inc(sM.h, 1)
                    @blk.vector
                    def _(v):
                        for h in range(4):
                            for gi_ in range(4):
                                idx = h * 4 + gi_
                                pso = psA if idx % 2 == 0 else psB
                                for ot in range(2):
                                    v.wait_ge(sM.h, bM + 2 * idx + ot + 1)
                                    col = gi_ * 8 + h * 2 + ot
                                    v.tensor_scalar_add(
                                        wxb[:, :, col],
                                        pso[:, ot * 512:ot * 512 + 256],
                                        cbl[:, l, col:col + 1]).then_inc(sV.h, 1)
                sL.n = bL + 256; sM.n = bM + 32; sV.n = bV + 32
                BAR()

                bM, bV, bS_ = sMs.n, sVs.n, sSs.n
                id0 = nc.next_id()
                ls, le = f"scan{id0}l", f"scan{id0}e"
                nc.regs_mov(slp, 0)
                nc.regs_mov(r_vT, bV); nc.regs_mov(r_vS, bV + 1)
                nc.regs_mov(r_mV, bM + 1); nc.regs_mov(r_sV, bS_ + 1)
                nc.regs_mov(r_iw, 0); nc.regs_mov(r_ih, 0)
                RVx = bass.RuntimeValue
                nc.br(ls, engines=ENG3)
                with nc.body(ls, valid_engines=ENG3):
                    nc.tensor.wait_ge(sVs.h, RVx(r_vT, min_val=bV,
                                                 max_val=bV + 510))
                    for h in range(4):
                        for mt in range(8):
                            col = (mt // 2) * 8 + h * 2 + (mt % 2)
                            for kt in range(2):
                                inst = nc.tensor.matmul(
                                    psA[:, col:col + 1],
                                    rkT[:, h, kt, mt, :],
                                    hbuf[:, h * 2 + kt:h * 2 + kt + 1],
                                    start=(kt == 0), stop=(kt == 1))
                    inst.then_inc(sMs.h, 1)
                    nc.regs_alu(r_vT, r_vT, 2, op=ALU.add)
                    nc.vector.wait_ge(sMs.h, RVx(r_mV, min_val=bM + 1,
                                                 max_val=bM + 256))
                    nc.regs_alu(r_mV, r_mV, 1, op=ALU.add)
                    g_ = sscr[:, 0:32]
                    nc.vector.tensor_add(g_, psA[:, 0:32], wxb[:, iv_w, :]
                                         ).then_inc(sVs.h, 1)
                    nc.regs_alu(r_iw, r_iw, 1, op=ALU.add)
                    eI = sscr[:, 32:40]; tF = sscr[:, 40:48]
                    tz = sscr[:, 48:56]; tO = sscr[:, 56:64]
                    nc.scalar.wait_ge(sVs.h, RVx(r_vS, min_val=bV + 1,
                                                 max_val=bV + 511))
                    nc.scalar.activation(eI, g_[:, 0:8], AF.Exp)
                    inst = nc.scalar.activation(sscr[:, 40:64], g_[:, 8:32],
                                                AF.Tanh)
                    inst.then_inc(sSs.h, 1)
                    nc.regs_alu(r_vS, r_vS, 2, op=ALU.add)
                    a_ = sst[:, 0:8]; E_ = sst[:, 8:16]
                    u_ = sscr[:, 64:72]; F_ = sscr[:, 72:80]
                    sg_ = sscr[:, 80:88]; Em = sscr[:, 88:96]
                    q_ = sscr[:, 96:104]; s_ = sscr[:, 104:112]
                    r_ = sscr[:, 112:120]
                    H2 = lambda t, h: t[:, 2 * h:2 * h + 2]
                    nc.vector.wait_ge(sSs.h, RVx(r_sV, min_val=bS_ + 1,
                                                 max_val=bS_ + 256))
                    nc.regs_alu(r_sV, r_sV, 1, op=ALU.add)
                    nc.vector.tensor_scalar(F_, tF, 0.5, 0.5,
                                            op0=ALU.mult, op1=ALU.add)
                    nc.vector.tensor_scalar(sg_, tO, 0.5, 0.5,
                                            op0=ALU.mult, op1=ALU.add)
                    nc.vector.tensor_mul(u_, eI, tz)
                    for h in range(4):
                        nc.vector.tensor_mul(H2(Em, h), H2(F_, h), H2(E_, h))
                    for h in range(4):
                        nc.vector.tensor_mul(H2(q_, h), H2(a_, h), H2(Em, h))
                    for h in range(4):
                        nc.vector.tensor_add(H2(E_, h), H2(Em, h), H2(eI, h))
                    for h in range(4):
                        nc.vector.tensor_add(H2(s_, h), H2(q_, h), H2(u_, h))
                    for h in range(4):
                        nc.vector.reciprocal(H2(r_, h), H2(E_, h))
                    for h in range(4):
                        nc.vector.tensor_mul(H2(a_, h), H2(s_, h), H2(r_, h))
                    for h in range(4):
                        nc.vector.tensor_mul(H2(hbuf[:, :], h), H2(sg_, h),
                                             H2(a_, h))
                    inst = nc.vector.tensor_mul(hv[:, iv_h, :], sg_, a_)
                    inst.then_inc(sVs.h, 1)
                    nc.regs_alu(r_ih, r_ih, 1, op=ALU.add)
                    nc.regs_alu(slp, slp, 1, op=ALU.add)
                    nc.br_lt(slp, 256, on_true=ls, on_false=le, engines=ENG3)
                nc.switch_bb(le)
                free_tmps(id0, nc.next_id())
                sMs.n = bM + 256; sVs.n = bV + 512; sSs.n = bS_ + 256
                with nc.Block() as blk:
                    @blk.vector
                    def _(v):
                        v.wait_ge(sVs.h, sVs.n)
                BAR()

                gnb = fscBb[:, 0:2048].rearrange("p (a c) -> p a c", a=2)
                bT = sT.n
                with nc.Block() as blk:
                    @blk.sync
                    def _(sy):
                        for h2o in range(8):
                            sy.dma_start_transpose(
                                gnb[:, :, h2o * 128:h2o * 128 + 128],
                                hns[:, h2o, :]).then_inc(sT.h, 16)
                    @blk.vector
                    def _(v):
                        v.wait_ge(sT.h, bT + 128)
                    @blk.scalar
                    def _(s):
                        s.wait_ge(sT.h, bT + 128)
                sT.n = bT + 128
                musum = smalls[:, 0:8]; s2c = smalls[:, 16:24]
                muc = smalls[:, 32:40]; varc = smalls[:, 48:56]
                rstdc = smalls[:, 64:72]; lvc = smalls[:, 80:88]
                bV, bS_ = sV.n, sS.n
                with nc.Block() as blk:
                    @blk.vector
                    def _(v):
                        for u in range(8):
                            tt2, h = u // 4, u % 4
                            inst = v.reduce_sum(musum[:, u:u + 1],
                                                gnb[:, tt2, h * 256:(h + 1) * 256],
                                                axis=AX)
                        inst.then_inc(sV.h, 1)
                    @blk.scalar
                    def _(s):
                        for u in range(8):
                            tt2, h = u // 4, u % 4
                            inst = s.activation(junk[:, 0:256],
                                                gnb[:, tt2, h * 256:(h + 1) * 256],
                                                AF.Square, accum_out=s2c[:, u:u + 1])
                        inst.then_inc(sS.h, 1)
                with nc.Block() as blk:
                    @blk.vector
                    def _(v):
                        v.wait_ge(sV.h, bV + 1)
                        v.wait_ge(sS.h, bS_ + 1)
                        v.tensor_scalar_mul(muc, musum, 1.0 / 256).then_inc(sV.h, 1)
                        v.wait_ge(sV.h, bV + 2)
                        v.tensor_mul(varc, muc, muc).then_inc(sV.h, 1)
                        v.wait_ge(sV.h, bV + 3)
                        v.scalar_tensor_tensor(varc, s2c, 1.0 / 256, varc,
                                               op0=ALU.mult, op1=ALU.subtract
                                               ).then_inc(sV.h, 1)
                    @blk.scalar
                    def _(s):
                        s.wait_ge(sV.h, bV + 4)
                        s.activation(lvc, varc, AF.Ln, bias=consts[:, 3:4]
                                     ).then_inc(sS.h, 1)
                        s.wait_ge(sS.h, bS_ + 2)
                        s.activation(rstdc, lvc, AF.Exp, scale=-0.5
                                     ).then_inc(sS.h, 1)
                bV2, bS2 = bV + 4, bS_ + 3
                with nc.Block() as blk:
                    @blk.vector
                    def _(v):
                        v.wait_ge(sS.h, bS2)
                        for u in range(8):
                            tt2, h = u // 4, u % 4
                            v.tensor_scalar(fscA[:, 0:256],
                                            gnb[:, tt2, h * 256:(h + 1) * 256],
                                            muc[:, u:u + 1], rstdc[:, u:u + 1],
                                            op0=ALU.subtract, op1=ALU.mult
                                            ).then_inc(sV.h, 1)
                            v.wait_ge(sV.h, bV2 + 3 * u + 1)
                            v.tensor_mul(fscA[:, 256:512], fscA[:, 0:256],
                                         gnwr[:, h * 256:(h + 1) * 256]
                                         ).then_inc(sV.h, 1)
                            v.wait_ge(sV.h, bV2 + 3 * u + 2)
                            v.tensor_add(x_res[:, 2 * ci + tt2,
                                               h * 256:(h + 1) * 256],
                                         x_res[:, 2 * ci + tt2,
                                               h * 256:(h + 1) * 256],
                                         fscA[:, 256:512]).then_inc(sV.h, 1)
                sV.n = bV2 + 24; sS.n = bS2
                BAR()

            bL = sL.n
            with nc.Block() as blk:
                @blk.gpsimd
                def _(g):
                    for kt in range(8):
                        g.dma_start(out=fscBb[:, 0:2816],
                                    in_=ff_up_h.ap()[l, kt * 128:(kt + 1) * 128, :]
                                    ).then_inc(sL.h, 16)
                        g.wait_ge(sL.h, bL + 32 * kt + 16)
                        g.dma_start(out=ffu_hbm.ap()[kt * 128:(kt + 1) * 128, :],
                                    in_=fscBb[:, 0:2816]).then_inc(sL.h, 16)
                        g.wait_ge(sL.h, bL + 32 * kt + 32)
                    for jt in range(11):
                        o0 = bL + 256 + 32 * jt
                        g.dma_start(out=fscBb[:, 0:1024],
                                    in_=ff_down_h.ap()[l, jt * 128:(jt + 1) * 128, :]
                                    ).then_inc(sL.h, 16)
                        g.wait_ge(sL.h, o0 + 16)
                        g.dma_start(out=ffd_hbm.ap()[jt * 128:(jt + 1) * 128, :],
                                    in_=fscBb[:, 0:1024]).then_inc(sL.h, 16)
                        g.wait_ge(sL.h, o0 + 32)
            sL.n = bL + 608
            BAR()
            emit_ln(x_res, apply_mode="copy", dst3=ytok)
            emit_t2d(ytok, xnT)
            bS_ = sS.n
            with nc.Block() as blk:
                @blk.scalar
                def _(s):
                    for dt in range(NDT):
                        inst = s.activation(xnT[:, dt, :], xnT[:, dt, :],
                                            AF.Identity,
                                            scale=lnw_sb[:, 1, l, dt:dt + 1])
                    inst.then_inc(sS.h, 1)
            sS.n = bS_ + 1
            BAR()
            wgj2 = [arena[:, 8192:9216].rearrange("p (k c) -> p k c", k=8),
                    arena[:, 11776:12800].rearrange("p (k c) -> p k c", k=8)]
            wvj2 = [arena[:, 9216:10240].rearrange("p (k c) -> p k c", k=8),
                    arena[:, 12800:13824].rearrange("p (k c) -> p k c", k=8)]
            wdj2 = [arena[:, 10240:11264], arena[:, 13824:14848]]
            ggj = arena[:, 11264:11520]
            prodj = arena[:, 11520:11776]
            for ci in range(8):
                t0 = ci * 256
                bL, bM, bS_, bV = sL.n, sM.n, sS.n, sV.n
                with nc.Block() as blk:
                    @blk.gpsimd
                    def _(g):
                        for j in range(11):
                            if j >= 2:
                                g.wait_ge(sM.h, bM + 3 * (j - 1))
                            g.dma_start(out=wgj2[j % 2][:, :, :],
                                        in_=ffu_hbm.ap()[:, j * 128:(j + 1) * 128]
                                        .rearrange("(k p) c -> p k c", p=128)
                                        ).then_inc(sL.h, 16)
                            g.dma_start(out=wvj2[j % 2][:, :, :],
                                        in_=ffu_hbm.ap()[:, UPP + j * 128:
                                                         UPP + (j + 1) * 128]
                                        .rearrange("(k p) c -> p k c", p=128)
                                        ).then_inc(sL.h, 16)
                            g.dma_start(out=wdj2[j % 2][:, :],
                                        in_=ffd_hbm.ap()[j * 128:(j + 1) * 128, :]
                                        ).then_inc(sL.h, 16)
                    @blk.tensor
                    def _(t):
                        for bank in range(4):
                            t.matmul(psB[:, bank * 512:(bank + 1) * 512], zeros128,
                                     xnT[:, 0, 0:512], start=True, stop=False,
                                     skip_group_check=True)
                        for j in range(11):
                            t.wait_ge(sL.h, bL + 48 * (j + 1))
                            if j >= 1:
                                t.wait_ge(sS.h, bS_ + j)
                                t.wait_ge(sV.h, bV + j)
                            for kt in range(8):
                                t.matmul(psA[:, 0:256], wgj2[j % 2][:, kt, :],
                                         xnT[:, kt, t0:t0 + 256],
                                         start=(kt == 0), stop=(kt == 7))
                            for kt in range(8):
                                inst = t.matmul(psA[:, 512:768],
                                                wvj2[j % 2][:, kt, :],
                                                xnT[:, kt, t0:t0 + 256],
                                                start=(kt == 0), stop=(kt == 7))
                            inst.then_inc(sM.h, 1)
                            t.wait_ge(sV.h, bV + j + 1)
                            for mt in range(8):
                                inst = t.matmul(psB[:, mt * 256:(mt + 1) * 256],
                                                wdj2[j % 2][:, mt * 128:
                                                            (mt + 1) * 128],
                                                prodj[:, 0:256],
                                                start=False, stop=(j == 10),
                                                skip_group_check=True)
                            inst.then_inc(sM.h, 2)
                    @blk.scalar
                    def _(s):
                        for j in range(11):
                            s.wait_ge(sM.h, bM + 3 * j + 1)
                            s.activation(ggj[:, 0:256], psA[:, 0:256], AF.Gelu
                                         ).then_inc(sS.h, 1)
                        s.wait_ge(sM.h, bM + 33)
                        for mt in range(8):
                            s.activation(xnT[:, mt, t0:t0 + 256],
                                         psB[:, mt * 256:(mt + 1) * 256],
                                         AF.Identity).then_inc(sS.h, 1)
                    @blk.vector
                    def _(v):
                        for j in range(11):
                            v.wait_ge(sS.h, bS_ + j + 1)
                            v.tensor_mul(prodj[:, 0:256], ggj[:, 0:256],
                                         psA[:, 512:768]).then_inc(sV.h, 1)
                sL.n = bL + 528; sM.n = bM + 33; sS.n = bS_ + 19; sV.n = bV + 11
                BAR()
            emit_d2t(xnT, ytok)
            bV = sV.n
            with nc.Block() as blk:
                @blk.vector
                def _(v):
                    for tt in range(NTT):
                        inst = v.tensor_add(x_res[:, tt, :], x_res[:, tt, :],
                                            ytok[:, tt, :])
                    inst.then_inc(sV.h, 1)
            sV.n = bV + 1
            BAR()

        pw_rep = gnwr
        bL = sL.n
        with nc.Block() as blk:
            @blk.gpsimd
            def _(g):
                g.dma_start(out=pw_rep[:, :],
                            in_=post_wh.ap().to_broadcast([128, D])
                            ).then_inc(sL.h, 16)
                g.wait_ge(sL.h, bL + 16)
        sL.n = bL + 16
        BAR()
        emit_ln(x_res, apply_mode="inplace")
        bV = sV.n
        with nc.Block() as blk:
            @blk.vector
            def _(v):
                for tt in range(NTT):
                    inst = v.tensor_mul(x_res[:, tt, :], x_res[:, tt, :],
                                        pw_rep[:, :])
                inst.then_inc(sV.h, 1)
        sV.n = bV + 1
        BAR()
        emit_ln(x_res, apply_mode="copy", dst3=ytok)
        bL, bV = sL.n, sV.n
        with nc.Block() as blk:
            @blk.gpsimd
            def _(g):
                for tt in range(NTT):
                    if tt >= 2:
                        g.wait_ge(sV.h, bV + tt - 1)
                    g.dma_start(out=fscA[:, 0:D] if tt % 2 == 0 else fscB[:, 0:D],
                                in_=xresv_hbm.ap()[:, tt * D:(tt + 1) * D]
                                ).then_inc(sL.h, 16)
            @blk.vector
            def _(v):
                for tt in range(NTT):
                    v.wait_ge(sL.h, bL + 16 * (tt + 1))
                    xs = fscA[:, 0:D] if tt % 2 == 0 else fscB[:, 0:D]
                    v.tensor_add(x_res[:, tt, :], xs, ytok[:, tt, :]
                                 ).then_inc(sV.h, 1)
        sL.n = bL + 16 * NTT; sV.n = bV + NTT
        BAR()

    def emit_store():
        with nc.Block() as blk:
            @blk.scalar
            def _(s):
                s.dma_start(out=out_hbm.ap().rearrange("(a p) d -> p a d", p=128),
                            in_=x_res[:]).then_inc(sO.h, 16)
                s.wait_ge(sO.h, sO.n + 16)
        sO.n += 16

    emit_l0_attn()
    if upto != "l0attn":
        emit_l0_ffn()
    if upto in ("l1attn", "all"):
        emit_l1_attn()
    if upto in ("l1ffn", "all", "skipl1a"):
        emit_l1_ffn()
    if upto == "qdbg":
        emit_qdbg()
    emit_store()
    st.close()
    return nc


def kernel(**inputs):
    prep = _prep(inputs)
    x_full = np.ascontiguousarray(np.asarray(inputs["L0_attn_x"], np.float32))
    nc = build_program(_UPTO)
    in_maps = []
    for c in range(8):
        m = dict(prep)
        m["x"] = np.ascontiguousarray(x_full[c])
        in_maps.append(m)
    res = run_bass_kernel_spmd(nc, in_maps, core_ids=list(range(8)))
    return np.stack([r["out"] for r in res.results]).astype(np.float32)
